# version 21
# baseline (speedup 1.0000x reference)
import numpy as np
import concourse.bass as bass
import concourse.mybir as mybir
from concourse.bass_utils import run_bass_kernel_spmd
from contextlib import ExitStack

F32 = mybir.dt.float32
BF16 = mybir.dt.bfloat16
I32 = mybir.dt.int32
AF = mybir.ActivationFunctionType
ALU = mybir.AluOpType
AX = mybir.AxisListType

E = 1024
SQ = 2048
H = 16
KD = 64
FF = 4096
DC = 8
HC = 8
FC = 32
TT = 512
NT = SQ // TT
NB = 2
VW = 65
NEG_BIG = -3.0e38
EPS_Q = 1e-8


class Sem:
    def __init__(self, ctx, nc, name):
        self.h = ctx.enter_context(nc.semaphore(name))
        self.n = 0


def build_program(taps=None):
    nc = bass.Bass("TRN2", target_bir_lowering=False)

    x_h = nc.dram_tensor("x", [NB, E, SQ], F32, kind="ExternalInput").ap()
    wq_h = nc.dram_tensor("wq", [E, E], F32, kind="ExternalInput").ap()
    wk_h = nc.dram_tensor("wk", [E, E], F32, kind="ExternalInput").ap()
    wv_h = nc.dram_tensor("wv", [E, E], F32, kind="ExternalInput").ap()
    wot_h = nc.dram_tensor("wot", [E, E], F32, kind="ExternalInput").ap()
    w1_h = nc.dram_tensor("w1", [FF, E], F32, kind="ExternalInput").ap()
    w2_h = nc.dram_tensor("w2", [E, FF], F32, kind="ExternalInput").ap()
    b1_h = nc.dram_tensor("b1", [128, FC], F32, kind="ExternalInput").ap()
    b2_h = nc.dram_tensor("b2", [128, DC], F32, kind="ExternalInput").ap()
    c_h = nc.dram_tensor("cvec", [NB, E], F32, kind="ExternalInput").ap()
    wmod_h = nc.dram_tensor("wmod", [E, 6 * E], F32, kind="ExternalInput").ap()
    xo_h = nc.dram_tensor("xo", [NB, E, SQ], F32, kind="ExternalOutput").ap()
    x1s = nc.dram_tensor("x1s", [NB, E, SQ], F32).ap()
    h2s = nc.dram_tensor("h2s", [NB, E, SQ], F32).ap()
    a_s = nc.dram_tensor("a_s", [NB * NT, FF, TT], F32).ap()
    m1s = nc.dram_tensor("m1s", [FF, E], BF16).ap()
    m2s = nc.dram_tensor("m2s", [E, FF], BF16).ap()
    cc1i = nc.dram_tensor("cc1i", [1, 128], F32).ap()
    cc1o = nc.dram_tensor("cc1o", [1, 128], F32).ap()
    cc2i = nc.dram_tensor("cc2i", [1, 128], F32).ap()
    cc2o = nc.dram_tensor("cc2o", [1, 128], F32).ap()

    tap_out = {}
    if taps:
        for name, (shape, dt) in taps.items():
            tap_out[name] = nc.dram_tensor("tap_" + name, list(shape), dt,
                                           kind="ExternalOutput").ap()

    with ExitStack() as ctx:
        sb = lambda n, s, d=F32: ctx.enter_context(nc.sbuf_tensor(n, s, d))

        mod_sb = sb("mod_sb", [128, 96])
        ones_f = sb("ones_f", [128, 128])
        ones_bf = sb("ones_bf", [128, 1], BF16)
        hsel_bf = sb("hsel_bf", [128, 2], BF16)
        idm = sb("idm", [128, 128])
        umask = sb("umask", [128, 128], BF16)
        siluT = sb("siluT", [128, DC * NB])
        cbias = sb("cbias", [128, NB * NT])
        nqacc = sb("nqacc", [2, 1])
        nkacc = sb("nkacc", [2, 1])
        ntmp = sb("ntmp", [2, 1])
        nrow = sb("nrow", [1, 4])
        uqa = sb("uqa", [128, 512])
        qia = sb("qia", [128, 512], I32)
        qfa = sb("qfa", [128, 512], BF16)
        qga = sb("qga", [128, 512], BF16)
        wst = sb("wst", [128, 512])
        mq1 = nc.alloc_sbuf_tensor_at("mq1", [128, 512], BF16,
                                      offset=nc.lookup_mloc(wst).addr)
        m1acc = sb("m1acc", [128, 1])
        m2acc = sb("m2acc", [128, 1])
        redt = sb("redt", [128, 1])
        cc_sb = uqa[0:1, 64:192]
        g1_sb = cc_sb
        srow1 = sb("srow1", [1, 2])
        ceps = sb("ceps", [128, 5])
        rcp1b = sb("rcp1b", [128, 1])
        s1b = sb("s1b", [128, 1])
        rcp2b = sb("rcp2b", [128, 1])
        s2b = sb("s2b", [128, 1])
        sw1c = sb("sw1c", [128, FC])
        sw2c = sb("sw2c", [128, DC])
        sv1 = sb("sv1", [128, FC])
        sv2 = sb("sv2", [128, DC])
        sv2g = sb("sv2g", [128, NB * DC])
        b1_sb = sb("b1_sb", [128, FC])
        b2_sb = sb("b2_sb", [128, DC])
        b1q = sb("b1q", [128, FC])
        b2q = sb("b2q", [128, DC])
        b2qg = sb("b2qg", [128, NB * DC])
        btmp1 = uqa[:, 0:FC]
        btmp2 = qia[:, 0:FC]
        btmp3 = uqa[:, FC:2 * FC]
        btmp4 = sb("btmp4", [128, FC])
        sums_row = sb("sums_row", [128, TT])
        statr = sb("statr", [1, 2 * TT])
        ctxA = ExitStack()
        sbA = lambda n, s_, d=F32: ctxA.enter_context(nc.sbuf_tensor(n, s_, d))
        wqb = sbA("wqb", [128, DC * E], BF16)
        wkb = sbA("wkb", [128, DC * E], BF16)
        wvb = sbA("wvb", [128, DC * E], BF16)
        wob = sbA("wob", [128, DC * E], BF16)
        kbuf = sbA("kbuf", [128, HC * SQ], BF16)
        vbuf = sbA("vbuf", [128, (SQ // 128) * H * VW], BF16)
        x0t = sbA("x0t", [128, 2 * DC * TT])
        hb = sbA("hb", [128, DC * TT], BF16)
        oraw = sbA("oraw", [128, HC * TT], BF16)
        qt = sbA("qt", [128, HC * TT], BF16)
        pbf = sbA("pbf", [128, 2 * TT], BF16)
        xbfc = sbA("xbfc", [128, TT], BF16)
        xsqc = sbA("xsqc", [128, TT], BF16)
        qsqc = xsqc
        ut1 = sbA("ut1", [128, TT])
        h2c = nc.alloc_sbuf_tensor_at("h2c", [128, TT], F32,
                                      offset=nc.lookup_mloc(pbf).addr)
        _kaddr = nc.lookup_mloc(kbuf).addr
        csilu = nc.alloc_sbuf_tensor_at("csilu", [NB, E], F32, offset=_kaddr)
        wmods = [nc.alloc_sbuf_tensor_at("wmods%d" % i, [128, DC * 128], F32,
                                         offset=_kaddr + 4096 * (1 + i))
                 for i in range(4)]

        PS = [ctx.enter_context(nc.psum_tensor("ps%d" % i, [128, 512], F32))
              for i in range(8)]

        sT = Sem(ctx, nc, "sT")
        sV = Sem(ctx, nc, "sV")
        sS = Sem(ctx, nc, "sS")
        sG = Sem(ctx, nc, "sG")
        sLD = Sem(ctx, nc, "sLD")
        sGL = Sem(ctx, nc, "sGL")
        sSQ = Sem(ctx, nc, "sSQ")
        sCC = Sem(ctx, nc, "sCC")

        block = ctx.enter_context(nc.Block())

        pend = {}

        def _mk(engname, sem):
            pend.setdefault(engname, [])

            def emit(f, waits=(), inc=1, sem_override=None):
                wl = [(s_.h, v) for (s_, v) in waits if v > 0]
                tgt = sem_override if sem_override is not None else sem
                real = not getattr(f, "_noop", False)
                pend[engname].append((wl, f, tgt, inc if real else 0))
                if real:
                    tgt.n += inc
                if len(pend[engname]) >= 96:
                    flush_one(engname)
            return emit

        def flush_one(engname):
            lst = pend[engname]
            if not lst:
                return
            for ent in lst:
                def body(eng, ent=ent):
                    wl, f, tgt, inc = ent
                    for h_, v_ in wl:
                        eng.wait_ge(h_, v_)
                    r = f(eng)
                    if r is not None and inc:
                        r.then_inc(tgt.h, inc)
                getattr(block, engname)(body)
            pend[engname] = []

        def flush_all():
            for engname in pend:
                flush_one(engname)

        T = _mk("tensor", sT)
        V = _mk("vector", sV)
        S = _mk("scalar", sS)
        G = _mk("gpsimd", sG)
        SY = _mk("sync", sLD)
        GD = _mk("gpsimd", sGL)
        SQD = _mk("scalar", sSQ)

        def snap(*sems):
            return tuple((s, s.n) for s in sems)

        def vsw():
            V(_noopf(), waits=snap(sV))

        def _noopf():
            f = lambda e: None
            f._noop = True
            return f

        def barrier():
            w = snap(sT, sV, sS, sG) + ((sLD, sLD.n), (sGL, sGL.n), (sSQ, sSQ.n))
            for em in (T, V, S, G):
                em(_noopf(), waits=w)

        def tap(name, apf):
            if name in tap_out:
                dst = tap_out[name]
                w = snap(sT, sV, sS, sG)
                SQD(lambda e: e.dma_start(out=dst, in_=apf()), waits=w, inc=16)
                barrier()

        mcol = lambda v, dc, b: mod_sb[:, (v * 8 + dc) * 2 + b:(v * 8 + dc) * 2 + b + 1]

        SY(lambda e: e.dma_start(out=csilu[:, :], in_=c_h), inc=16)
        SY(lambda e: e.dma_start(out=b1_sb[:, :], in_=b1_h), inc=16)
        SY(lambda e: e.dma_start(out=b2_sb[:, :], in_=b2_h), inc=16)
        GD(lambda e: e.dma_start(
            out=wqb.ap().rearrange("p (c n) -> p c n", c=DC),
            in_=wq_h.rearrange("(c p) n -> p c n", p=128)), inc=16)
        GD(lambda e: e.dma_start(
            out=wkb.ap().rearrange("p (c n) -> p c n", c=DC),
            in_=wk_h.rearrange("(c p) n -> p c n", p=128)), inc=16)
        GD(lambda e: e.dma_start(
            out=wvb.ap().rearrange("p (c n) -> p c n", c=DC),
            in_=wv_h.rearrange("(c p) n -> p c n", p=128)), inc=16)
        GD(lambda e: e.dma_start(
            out=wob.ap().rearrange("p (c n) -> p c n", c=DC),
            in_=wot_h.rearrange("(c p) n -> p c n", p=128)), inc=16)

        V(lambda e: e.memset(ones_f[:, :], 1.0))
        V(lambda e: e.memset(ones_bf[:, :], 1.0))
        V(lambda e: e.memset(hsel_bf[:, :], 0.0))
        V(lambda e: e.memset(hsel_bf[0:64, 0:1], 1.0))
        V(lambda e: e.memset(hsel_bf[64:128, 1:2], 1.0))
        V(lambda e: e.memset(idm[:, :], 1.0))
        V(lambda e: e.memset(umask[:, :], 1.0))
        V(lambda e: e.memset(ceps[:, 0:1], 1e-5))
        V(lambda e: e.memset(ceps[:, 1:2], 1e-30))
        V(lambda e: e.memset(ceps[:, 2:3], 0.5))
        V(lambda e: e.memset(ceps[:, 3:4], 128.5))
        V(lambda e: e.memset(ceps[:, 4:5], 32768.5))
        V(lambda e: e.memset(nqacc[:, :], 0.0))
        V(lambda e: e.memset(nkacc[:, :], 0.0))
        V(lambda e: e.memset(m1acc[:, :], NEG_BIG))
        V(lambda e: e.memset(m2acc[:, :], NEG_BIG))
        V(lambda e: e.memset(
            vbuf.ap().rearrange("p (m h w) -> p m h w", m=SQ // 128, h=H)[:, :, :, 64:65],
            1.0))
        G(lambda e: e.affine_select(umask[:, :], umask[:, :], [[1, 128]],
                                    ALU.is_ge, 0.0, base=0, channel_multiplier=-1),
          waits=snap(sV))
        G(lambda e: e.affine_select(idm[:, :], idm[:, :], [[1, 128]],
                                    ALU.is_equal, 0.0, base=0, channel_multiplier=-1),
          waits=snap(sV))

        S(lambda e: e.activation(csilu[:, :], csilu[:, :], AF.Silu),
          waits=((sLD, 16),))
        wsil = snap(sS, sG)
        for dc in range(DC):
            T(lambda e, dc=dc: e.transpose(
                PS[0][0:128, 0:2], csilu[0:NB, dc * 128:(dc + 1) * 128],
                idm[0:NB, 0:NB]), waits=wsil + snap(sV))
            V(lambda e, dc=dc: e.tensor_copy(siluT[:, dc * 2:dc * 2 + 2],
                                             PS[0][0:128, 0:2]), waits=snap(sT))

        wm_T = [0, 0, 0, 0]
        pm_V = [0, 0, 0, 0]
        for mc in range(48):
            sl = mc % 4
            q_, sem_ = (SY, sLD) if mc % 2 == 0 else (SQD, sSQ)
            q_(lambda e, mc=mc, sl=sl: e.dma_start(
                out=wmods[sl].ap().rearrange("p (c n) -> p c n", c=DC),
                in_=wmod_h.rearrange("(c p) (m n) -> p m c n", p=128, n=128)[:, mc]),
               waits=((sT, wm_T[sl]),) + (snap(sV) if mc < 4 else ()), inc=16)
            wl = ((sem_, sem_.n), (sV, pm_V[sl]))
            for dc in range(DC):
                T(lambda e, mc=mc, dc=dc, sl=sl: e.matmul(
                    PS[5][0:128, sl * 2:sl * 2 + 2],
                    wmods[sl][:, dc * 128:dc * 128 + 128],
                    siluT[:, dc * 2:dc * 2 + 2],
                    start=(dc == 0), stop=(dc == DC - 1)),
                  waits=(wl + (snap(sV) if mc < 4 else ()) if dc == 0 else ()))
            wm_T[sl] = sT.n
            V(lambda e, mc=mc, sl=sl: e.tensor_copy(
                mod_sb[:, mc * 2:mc * 2 + 2],
                PS[5][0:128, sl * 2:sl * 2 + 2]), waits=snap(sT))
            pm_V[sl] = sV.n
        barrier()

        V(lambda e: e.tensor_scalar_add(mod_sb[:, 16:32], mod_sb[:, 16:32], 1.0))
        V(lambda e: e.tensor_scalar_add(mod_sb[:, 64:80], mod_sb[:, 64:80], 1.0))
        for lo in (32, 80):
            sl = mod_sb[:, lo:lo + 16]
            V(lambda e, sl=sl: e.tensor_single_scalar(btmp1[:, 0:16], sl, 0.0, ALU.is_ge))
            V(lambda e: e.tensor_scalar(btmp1[:, 0:16], btmp1[:, 0:16], 2.0, -1.0,
                                        ALU.mult, ALU.add))
            V(lambda e, sl=sl: e.scalar_tensor_tensor(btmp3[:, 0:16], sl, -1.0, sl,
                                                      ALU.mult, ALU.max))
            V(lambda e: e.tensor_scalar_max(btmp3[:, 0:16], btmp3[:, 0:16], 0.25))
            V(lambda e, sl=sl: e.tensor_mul(sl, btmp1[:, 0:16], btmp3[:, 0:16]))
        barrier()
        tap("mod", lambda: mod_sb[:, :])

        wstS = [0]
        wstV = [0]
        prep_plan = []
        for fc in range(FC):
            for s_i in range(2):
                prep_plan.append(("a", w1_h, fc * 128, s_i,
                                  sw1c[:, fc:fc + 1], s_i == 0))
            prep_plan.append(("f", sw1c[:, fc:fc + 1], btmp4[:, fc:fc + 1]))
        for fc in range(FC):
            for s_i in range(2):
                prep_plan.append(("q", w1_h, fc * 128, s_i,
                                  btmp4[:, fc:fc + 1], m1s))
        for ec in range(DC):
            for s_i in range(8):
                prep_plan.append(("a", w2_h, ec * 128, s_i,
                                  sw2c[:, ec:ec + 1], s_i == 0))
            prep_plan.append(("f", sw2c[:, ec:ec + 1],
                              btmp4[:, FC - DC + ec:FC - DC + ec + 1]))
        for ec in range(DC):
            for s_i in range(8):
                prep_plan.append(("q", w2_h, ec * 128, s_i,
                                  btmp4[:, FC - DC + ec:FC - DC + ec + 1], m2s))

        def emit_prep_quantum(*ent):
            kind = ent[0]
            if kind == "f":
                out_col, rcp_col = ent[1], ent[2]
                V(lambda e: e.tensor_scalar(out_col, out_col,
                                            EPS_Q, 1.0 / 127.0, ALU.max, ALU.mult))
                vsw()
                V(lambda e: e.reciprocal(rcp_col, out_col))
                vsw()
                return
            if kind == "a":
                _, w_h, row0, s_i, out_col, first = ent
                GD(lambda e, w_h=w_h, row0=row0, s_i=s_i: e.dma_start(
                    out=wst[:, :],
                    in_=w_h[row0:row0 + 128, s_i * 512:(s_i + 1) * 512]),
                   waits=((sS, wstS[0]), (sV, wstV[0])), inc=16)
                wl = ((sGL, sGL.n),)
                V(lambda e: e.tensor_reduce(
                    redt[:, :], wst[:, :], AX.X, ALU.max,
                    apply_absolute_value=True), waits=wl)
                wstV[0] = sV.n
                vsw()
                if first:
                    V(lambda e: e.tensor_copy(out_col, redt[:, :]))
                else:
                    V(lambda e: e.tensor_max(out_col, out_col, redt[:, :]))
                return
            _, w_h, row0, s_i, rcp_col, dst = ent
            GD(lambda e, w_h=w_h, row0=row0, s_i=s_i: e.dma_start(
                out=wst[:, :],
                in_=w_h[row0:row0 + 128, s_i * 512:(s_i + 1) * 512]),
               waits=((sS, wstS[0]), (sV, wstV[0])) + ((sSQ, sSQ.n),), inc=16)
            wl = ((sGL, sGL.n),)
            S(lambda e: e.activation(uqa[:, 0:512], wst[:, :],
                                     AF.Identity, scale=rcp_col),
              waits=wl + snap(sV))
            wstS[0] = sS.n
            wS = snap(sS)
            V(lambda e: e.tensor_copy(qia[:, 0:512], uqa[:, 0:512]), waits=wS)
            V(lambda e: e.tensor_copy(qfa[:, 0:512], qia[:, 0:512]))
            V(lambda e: e.scalar_tensor_tensor(qga[:, 0:512], uqa[:, 0:512], -0.5,
                                               qfa[:, 0:512], ALU.add, ALU.is_ge))
            V(lambda e: e.scalar_tensor_tensor(qia[:, 0:512], uqa[:, 0:512], 0.5,
                                               qfa[:, 0:512], ALU.add, ALU.is_le))
            V(lambda e: e.tensor_sub(qga[:, 0:512], qga[:, 0:512], qia[:, 0:512]))
            V(lambda e: e.tensor_add(mq1[:, :], qfa[:, 0:512], qga[:, 0:512]),
              waits=((sSQ, sSQ.n),))
            wstV[0] = sV.n
            SQD(lambda e, row0=row0, s_i=s_i, dst=dst: e.dma_start(
                out=dst[row0:row0 + 128, s_i * 512:(s_i + 1) * 512],
                in_=mq1[:, :]),
                waits=snap(sV), inc=16)

        prep_iter = iter(prep_plan)

        last_stats_T = [0]
        prev_ln_done = [()]
        hb_ready = [0]
        hb_free_T = [0]
        cbound_T = [0]
        x0_free = {0: (), 1: ()}
        xload_mark = [0]
        h2mark = [0]
        psAB_V = [0]
        flashES = [0, 0]
        tailV = [0, 0]

        def xsrc(g):
            off = (g % 2) * DC * TT
            return lambda dc: x0t[:, off + dc * TT:off + (dc + 1) * TT]

        def emit_xload(g):
            b, t = divmod(g, NT)
            gen = g % 2
            SY(lambda e, b=b, t=t, gen=gen: e.dma_start(
                out=x0t.ap().rearrange("p (u c n) -> p u c n", u=2, c=DC)[:, gen],
                in_=x_h.rearrange("b (c p) (t n) -> b p t c n", p=128, n=TT)[b, :, t]),
               waits=x0_free[gen], inc=16)
            xload_mark[0] = sLD.n

        def ln_pieces(srcf, b, vs, vb, out_cb, out_done=lambda dc: None,
                      pre_out=lambda: (), first_wait=lambda: (),
                      out_gate=lambda: (), is_h=False):
            st = {"cur": 0, "bc": 0}

            def stats_chunk(dc):
                w = ((sT, last_stats_T[0] if dc == 0 else st["cur"]),)
                if dc == 0:
                    w = w + first_wait()
                V(lambda e, dc=dc: e.tensor_copy(xbfc[:, :], srcf(dc)), waits=w)
                V(lambda e, dc=dc: e.tensor_mul(xsqc[:, :], srcf(dc), srcf(dc)))
                wd = snap(sV) + (prev_ln_done[0] if dc == 0 else ())
                T(lambda e, dc=dc: e.matmul(PS[0][0:1, :], ones_bf[:, :], xbfc[:, :],
                                            start=(dc == 0), stop=(dc == DC - 1)),
                  waits=wd)
                T(lambda e, dc=dc: e.matmul(PS[1][0:1, :], ones_bf[:, :], xsqc[:, :],
                                            start=(dc == 0), stop=(dc == DC - 1)))
                st["cur"] = sT.n
                if dc == DC - 1:
                    last_stats_T[0] = sT.n

            mean_r = statr[0:1, 0:TT]
            msq_r = statr[0:1, TT:2 * TT]
            var_r = msq_r
            qn_r = mean_r

            def rows():
                wst_ = ((sT, st["cur"]),)
                V(lambda e: e.tensor_scalar_mul(mean_r, PS[0][0:1, :], 1.0 / E),
                  waits=wst_)
                vsw()
                V(lambda e: e.tensor_mul(msq_r, mean_r, mean_r))
                vsw()
                V(lambda e: e.scalar_tensor_tensor(var_r, PS[1][0:1, :], 1.0 / E,
                                                   msq_r, ALU.mult, ALU.subtract))
                vsw()
                wv_ = snap(sV)
                S(lambda e: e.activation(var_r, var_r, AF.Ln, bias=ceps[0:1, 0:1]),
                  waits=wv_)
                S(lambda e: e.activation(var_r, var_r, AF.Exp, scale=-0.5))
                wv2 = snap(sS)
                V(lambda e: e.tensor_mul(qn_r, mean_r, var_r), waits=wv2)
                vsw()
                V(lambda e: e.tensor_scalar_mul(qn_r, qn_r, -1.0))
                vsw()
                wv3 = snap(sV, sS)
                T(lambda e: e.matmul(PS[2][:, :], ones_f[0:1, :], var_r,
                                     start=True, stop=True), waits=wv3)
                T(lambda e: e.matmul(PS[3][:, :], ones_f[0:1, :], qn_r,
                                     start=True, stop=True))
                st["bc"] = sT.n

            def apply2(dc0):
                for dc in (dc0, dc0 + 1):
                    wbc = ((sT, st["bc"]),) + snap(sS)
                    V(lambda e, dc=dc: e.tensor_mul(ut1[:, :], srcf(dc),
                                                    PS[2][:, :]), waits=wbc)
                    V(lambda e: e.tensor_add(ut1[:, :], ut1[:, :], PS[3][:, :]))
                    wu = pre_out() + (out_gate() if dc == 0 else ())
                    V(lambda e, dc=dc: e.tensor_scalar(
                        ut1[:, :], ut1[:, :], mcol(vs, dc, b), None, ALU.mult),
                      waits=wu)
                    V(lambda e, dc=dc: e.tensor_scalar(
                        out_cb(dc), ut1[:, :], mcol(vb, dc, b), None, ALU.add))
                    out_done(dc)
                    if dc == DC - 1:
                        prev_ln_done[0] = snap(sV, sS)
                        if is_h:
                            hb_ready[0] = sV.n

            return ([lambda dc=dc: stats_chunk(dc) for dc in range(DC)]
                    + [rows] + [lambda d=d: apply2(d) for d in (0, 2, 4, 6)])

        def emit_qkv(g):
            b, t = divmod(g, NT)
            wh = ((sV, hb_ready[0]),)
            qkvV = {0: psAB_V[0], 1: psAB_V[0]}
            for wname, wb_ in (("q", wqb), ("k", wkb)):
                for hc in range(HC):
                    pb = PS[4 + hc % 2]
                    wfree = ((sV, qkvV[hc % 2]),)
                    for dc in range(DC):
                        T(lambda e, wb_=wb_, hc=hc, dc=dc, pb=pb: e.matmul(
                            pb[:, :],
                            wb_[:, dc * E + hc * 128:dc * E + hc * 128 + 128],
                            hb[:, dc * TT:(dc + 1) * TT],
                            start=(dc == 0), stop=(dc == DC - 1)),
                          waits=(wh + wfree if dc == 0 else ()))
                    wmm = snap(sT)
                    if wname == "q":
                        V(lambda e, hc=hc, pb=pb: e.tensor_copy(
                            qt[:, hc * TT:(hc + 1) * TT], pb[:, :]), waits=wmm)
                        qkvV[hc % 2] = sV.n
                        V(lambda e, hc=hc: e.tensor_mul(
                            qsqc[:, :], qt[:, hc * TT:(hc + 1) * TT],
                            qt[:, hc * TT:(hc + 1) * TT]))
                        wq_ = snap(sV)
                        T(lambda e: e.matmul(PS[6][0:2, :], hsel_bf[:, :],
                                             qsqc[:, :], start=True, stop=True),
                          waits=wq_)
                        wn = snap(sT)
                        V(lambda e: e.tensor_reduce(ntmp[:, :], PS[6][0:2, :],
                                                    AX.X, ALU.max), waits=wn)
                        vsw()
                        V(lambda e: e.tensor_max(nqacc[:, :], nqacc[:, :], ntmp[:, :]))
                    else:
                        V(lambda e, b=b, t=t, hc=hc, pb=pb: e.tensor_copy(
                            kbuf[:, hc * SQ + t * TT:hc * SQ + (t + 1) * TT],
                            pb[:, :]), waits=wmm)
                        qkvV[hc % 2] = sV.n
                        V(lambda e, hc=hc, b=b, t=t: e.tensor_mul(
                            qsqc[:, :], kbuf[:, hc * SQ + t * TT:hc * SQ + (t + 1) * TT],
                            kbuf[:, hc * SQ + t * TT:hc * SQ + (t + 1) * TT]))
                        wq_ = snap(sV)
                        T(lambda e: e.matmul(PS[7][0:2, :], hsel_bf[:, :],
                                             qsqc[:, :], start=True, stop=True),
                          waits=wq_)
                        wn = snap(sT)
                        V(lambda e: e.tensor_reduce(ntmp[:, :], PS[7][0:2, :],
                                                    AX.X, ALU.max), waits=wn)
                        vsw()
                        V(lambda e: e.tensor_max(nkacc[:, :], nkacc[:, :], ntmp[:, :]))
            for tsub in range(4):
                for half in range(2):
                    par = (tsub * 2 + half) % 2
                    pb = PS[4 + par]
                    wfree = ((sV, qkvV[par]),)
                    for dc in range(DC):
                        T(lambda e, tsub=tsub, half=half, dc=dc, pb=pb: e.matmul(
                            pb[:, :],
                            hb[:, dc * TT + tsub * 128:dc * TT + tsub * 128 + 128],
                            wvb[:, dc * E + half * 512:dc * E + half * 512 + 512],
                            start=(dc == 0), stop=(dc == DC - 1)),
                          waits=(wfree if dc == 0 else ()))
                    wmm = snap(sT)
                    mc = t * 4 + tsub
                    V(lambda e, mc=mc, half=half, pb=pb: e.tensor_copy(
                        vbuf.ap().rearrange(
                            "p (m h w) -> p m h w", m=SQ // 128, h=H
                        )[:, mc, half * 8:half * 8 + 8, 0:64],
                        pb.ap().rearrange("p (h k) -> p h k", h=8)),
                      waits=wmm)
                    qkvV[par] = sV.n
            hb_free_T[0] = sT.n

        def emit_cbound(g):
            b, t = divmod(g, NT)
            wnn = snap(sV)
            T(lambda e: e.transpose(PS[6][0:1, 0:2], nqacc[:, :], idm[0:2, 0:2]),
              waits=wnn)
            T(lambda e: e.transpose(PS[7][0:1, 0:2], nkacc[:, :], idm[0:2, 0:2]))
            cbound_T[0] = sT.n
            wtr = snap(sT)
            V(lambda e: e.reduce_max(nrow[0:1, 0:1], PS[6][0:1, 0:2], AX.X), waits=wtr)
            V(lambda e: e.reduce_max(nrow[0:1, 1:2], PS[7][0:1, 0:2], AX.X))
            vsw()
            wr = snap(sV)
            S(lambda e: e.activation(nrow[0:1, 2:3], nrow[0:1, 0:1], AF.Ln,
                                     bias=ceps[0:1, 1:2]), waits=wr)
            S(lambda e: e.activation(nrow[0:1, 3:4], nrow[0:1, 1:2], AF.Ln,
                                     bias=ceps[0:1, 1:2]))
            ws_ = snap(sS)
            V(lambda e: e.tensor_add(nrow[0:1, 2:3], nrow[0:1, 2:3], nrow[0:1, 3:4]),
              waits=ws_)
            vsw()
            wr2 = snap(sV)
            S(lambda e: e.activation(nrow[0:1, 2:3], nrow[0:1, 2:3], AF.Exp,
                                     scale=0.5), waits=wr2)
            ws2 = snap(sS)
            V(lambda e: e.tensor_scalar_mul(nrow[0:1, 2:3], nrow[0:1, 2:3], -0.125),
              waits=ws2)
            vsw()
            wr3 = snap(sV)
            T(lambda e: e.matmul(PS[6][:, 0:1], ones_f[0:1, :], nrow[0:1, 2:3],
                                 start=True, stop=True), waits=wr3)
            wt2 = snap(sT)
            V(lambda e, g=g: e.tensor_copy(cbias[:, g:g + 1], PS[6][:, 0:1]),
              waits=wt2)

        def emit_flash(g, sched):
            b, t = divmod(g, NT)
            nmb = 4 * (t + 1)
            fsV = sV.n
            exp_sS = {0: 0, 1: 0}
            for hc in range(HC):
                for par in range(2):
                    po = PS[4 + par]
                    h = 2 * hc + par
                    p_i = hc * 2 + par
                    wpo = snap(sS, sV)
                    tmark = {}

                    def smm(mb, hc=hc, par=par, p_i=p_i, tmark=tmark):
                        rot = mb % 2
                        w = ((sS, exp_sS[rot]),)
                        if p_i == 0 and mb <= 1:
                            w = w + ((sV, fsV),)
                        T(lambda e, mb=mb, hc=hc, par=par, rot=rot: e.matmul(
                            PS[6 + rot][:, :],
                            kbuf[par * 64:par * 64 + 64,
                                 hc * SQ + mb * 128:hc * SQ + mb * 128 + 128],
                            qt[par * 64:par * 64 + 64, hc * TT:(hc + 1) * TT],
                            start=True, stop=True), waits=w)
                        tmark[mb] = sT.n

                    def eav(mb, p_i=p_i, po=po, h=h, wpo=wpo, tmark=tmark):
                        rot = mb % 2
                        wexp = ((sT, tmark[mb]),)
                        if p_i == 0:
                            wexp = wexp + ((sV, fsV),)
                            if mb <= 1:
                                wexp = wexp + ((sSQ, h2mark[0]),)
                        S(lambda e, rot=rot, g=g: e.activation(
                            pbf[:, rot * TT:(rot + 1) * TT], PS[6 + rot][:, :],
                            AF.Exp, scale=0.125, bias=cbias[:, g:g + 1]),
                          waits=wexp)
                        exp_sS[rot] = sS.n
                        flashES[rot] = sS.n
                        wmask = snap(sS)
                        need_v = False
                        if mb >= 4 * t:
                            off = (mb - 4 * t) * 128
                            if off > 0:
                                V(lambda e, rot=rot, off=off: e.memset(
                                    pbf[:, rot * TT:rot * TT + off], 0.0),
                                  waits=wmask)
                            V(lambda e, rot=rot, off=off: e.tensor_mul(
                                pbf[:, rot * TT + off:rot * TT + off + 128],
                                pbf[:, rot * TT + off:rot * TT + off + 128],
                                umask[:, :]), waits=wmask)
                            need_v = True
                        wav = snap(sV) if need_v else snap(sS)
                        if mb == 0:
                            wav = wav + wpo
                        T(lambda e, h=h, mb=mb, rot=rot, po=po, nmb=nmb: e.matmul(
                            po[0:VW, :],
                            vbuf[:, mb * (H * VW) + h * VW:
                                 mb * (H * VW) + h * VW + VW],
                            pbf[:, rot * TT:(rot + 1) * TT],
                            start=(mb == 0), stop=(mb == nmb - 1)), waits=wav)

                    smm(0)
                    for mb in range(1, nmb):
                        smm(mb)
                        eav(mb - 1)
                    eav(nmb - 1)
                    wev = snap(sT)
                    S(lambda e, par=par, po=po: e.copy(
                        sums_row[par * 64:par * 64 + 1, :], po[64:65, :]),
                      waits=wev)
                    S(lambda e, hc=hc, par=par, po=po: e.copy(
                        oraw[par * 64:par * 64 + 64, hc * TT:(hc + 1) * TT],
                        po[0:64, :]))
                    if par == 1:
                        wsr = snap(sS)
                        T(lambda e: e.matmul(
                            PS[4][0:64, :], ones_f[0:1, 0:64],
                            sums_row[0:1, :], start=True, stop=True),
                          waits=wsr + snap(sV))
                        T(lambda e: e.matmul(
                            PS[4][64:128, :], ones_f[64:65, 0:64],
                            sums_row[64:65, :], start=True, stop=True))
                        wbc2 = snap(sT)
                        V(lambda e: e.reciprocal(PS[5][:, :], PS[4][:, :]), waits=wbc2)
                        vsw()
                        V(lambda e, hc=hc: e.tensor_mul(
                            oraw[:, hc * TT:(hc + 1) * TT],
                            oraw[:, hc * TT:(hc + 1) * TT], PS[5][:, :]))
                        psAB_V[0] = sV.n
                    for f_ in sched.get(p_i, ()):
                        f_()

        def emit_tail(g):
            b, t = divmod(g, NT)
            gen = g % 2
            off = gen * DC * TT

            def h2_done(dc, b=b, t=t):
                wd = snap(sV)
                V(lambda e: e.reduce_max(redt[:, :], h2c[:, :], AX.X), waits=wd)
                vsw()
                V(lambda e: e.tensor_max(m1acc[:, :], m1acc[:, :], redt[:, :]))
                SQD(lambda e, dc=dc, b=b, t=t: e.dma_start(
                    out=h2s.rearrange("b (c p) (t n) -> b p c t n",
                                      p=128, n=TT)[b, :, dc, t],
                    in_=h2c[:, :]), waits=wd, inc=16)

            tailT = [0]
            ln2 = ln_pieces(xsrc(g), b, 4, 3, lambda dc: h2c[:, :],
                            out_done=h2_done,
                            pre_out=lambda: ((sSQ, sSQ.n),),
                            out_gate=lambda: ((sT, tailT[0]),))

            wor = ((sV, psAB_V[0]),)
            for ec in range(DC):
                px = PS[6 + ec % 2]
                wfree = ((sS, flashES[ec % 2]), (sV, tailV[ec % 2]))
                for hc in range(HC):
                    T(lambda e, ec=ec, hc=hc, px=px: e.matmul(
                        px[:, :],
                        wob[:, hc * E + ec * 128:hc * E + ec * 128 + 128],
                        oraw[:, hc * TT:(hc + 1) * TT],
                        start=(hc == 0), stop=(hc == HC - 1)),
                      waits=(wor + wfree if hc == 0 else ()))
                wpx = snap(sT)
                V(lambda e, ec=ec, b=b, px=px, off=off: e.scalar_tensor_tensor(
                    x0t[:, off + ec * TT:off + (ec + 1) * TT], px[:, :],
                    mcol(2, ec, b),
                    x0t[:, off + ec * TT:off + (ec + 1) * TT],
                    ALU.mult, ALU.add), waits=wpx)
                tailV[ec % 2] = sV.n
                ln2[ec]()
            wx1 = snap(sV)
            SQD(lambda e, b=b, t=t, gen=gen: e.dma_start(
                out=x1s.rearrange("b (c p) (t n) -> b p t c n", p=128, n=TT)[b, :, t],
                in_=x0t.ap().rearrange("p (u c n) -> p u c n", u=2, c=DC)[:, gen]),
                waits=wx1, inc=16)
            tailT[0] = sT.n
            for f_ in ln2[8:]:
                f_()
            h2mark[0] = sSQ.n
            x0_free[gen] = snap(sV) + ((sSQ, sSQ.n),)

        NTILE = NB * NT
        PREP_SLOTS = tuple(range(16))
        NPREP_TILE = -(-len(prep_plan) // NTILE)

        def ln1_for(g):
            b2 = g // NT
            return ln_pieces(xsrc(g), b2, 1, 0,
                             lambda dc: hb[:, dc * TT:(dc + 1) * TT],
                             first_wait=lambda: ((sLD, xload_mark[0]),),
                             out_gate=lambda: ((sT, hb_free_T[0]),), is_h=True)

        emit_xload(0)
        for f_ in ln1_for(0):
            f_()
        tap("h1_t0", lambda: hb[:, :])
        for g in range(NTILE):
            b, t = divmod(g, NT)
            if t == 0 and g > 0:
                wb_ = ((sT, cbound_T[0]),)
                V(lambda e: e.memset(nqacc[:, :], 0.0), waits=wb_)
                V(lambda e: e.memset(nkacc[:, :], 0.0))
            emit_qkv(g)
            emit_cbound(g)
            sched = {}
            if g + 1 < NTILE:
                pieces = ln1_for(g + 1)
                sched[0] = [lambda g=g: emit_xload(g + 1)]
                for i_ in range(13):
                    sched.setdefault(1 + i_, []).append(pieces[i_])
            for i_ in range(NPREP_TILE):
                nxt = next(prep_iter, None)
                if nxt is None:
                    break
                p_ = PREP_SLOTS[i_ * len(PREP_SLOTS) // NPREP_TILE]
                sched.setdefault(p_, []).append(
                    lambda nxt=nxt: emit_prep_quantum(*nxt))
            emit_flash(g, sched)
            if g == 0:
                tap("oraw_t0", lambda: oraw[:, :])
            emit_tail(g)
            if g == 0:
                tap("x1_t0", lambda: x0t[:, 0:DC * TT])
                tap("h2_t0", lambda: h2c[:, 0:TT])
            if g == NTILE - 1:
                tap("h2_last", lambda: h2c[:, 0:TT])
                tap("x1_last", lambda: x0t[:, DC * TT:2 * DC * TT])
                tap("cbias", lambda: cbias[:, :])

        for nxt in prep_iter:
            emit_prep_quantum(*nxt)
        barrier()
        tap("sw1c", lambda: sw1c[:, :])
        tap("m1acc", lambda: m1acc[:, :])
        ctxA.close()
        ctxB = ExitStack()
        sbB = lambda n, s_, d=F32: ctxB.enter_context(nc.sbuf_tensor(n, s_, d))
        m1t = sbB("m1t", [128, DC * FF], BF16)
        h2in = sbB("h2in", [128, 2 * DC * TT])
        nbuf = sbB("nbuf", [128, 2 * DC * TT], BF16)
        astg = sbB("astg", [128, 4 * 512])
        m2t = sbB("m2t", [128, FC * E], BF16)
        n2buf = sbB("n2buf", [128, 4 * TT], BF16)
        ain = nc.alloc_sbuf_tensor_at("ain", [128, 4 * 512], mybir.dt.float32,
                                      offset=nc.lookup_mloc(astg).addr)
        x1in = nc.alloc_sbuf_tensor_at("x1in", [128, DC * TT], mybir.dt.float32,
                                       offset=nc.lookup_mloc(nbuf).addr)
        tstg = nc.alloc_sbuf_tensor_at("tstg", [128, 2 * 512], mybir.dt.float32,
                                       offset=nc.lookup_mloc(h2in).addr)
        x2stg = nc.alloc_sbuf_tensor_at("x2stg", [128, 2 * 512], mybir.dt.float32,
                                        offset=nc.lookup_mloc(h2in).addr + 4096)
        for dc in range(DC):
            SY(lambda e, dc=dc: e.dma_start(
                out=m1t[:, dc * FF:(dc + 1) * FF],
                in_=m1s[:, dc * 128:(dc + 1) * 128], transpose=True),
               waits=((sSQ, sSQ.n),), inc=16)
        wload_m1 = ((sLD, sLD.n),)
        for fc in range(FC):
            SY(lambda e, fc=fc: e.dma_start(
                out=m2t[:, fc * E:(fc + 1) * E],
                in_=m2s[:, fc * 128:(fc + 1) * 128], transpose=True),
               inc=16)
        wload_m2 = ((sLD, sLD.n),)

        def emit_collective2(acc, cci, cco, sb_scal, rcpb, sbb):
            T(lambda e: e.transpose(PS[2][0:1, 0:128], acc[:, :], idm[:, :]),
              waits=snap(sV))
            wtr_ = snap(sT)
            V(lambda e: e.reduce_max(cc_sb[0:1, 0:1], PS[2][0:1, 0:128], AX.X),
              waits=wtr_)
            vsw()
            V(lambda e: e.memset(cc_sb[0:1, 1:128], NEG_BIG))
            wv_ = snap(sV)
            GD(lambda e: e.dma_start(out=cci, in_=cc_sb[:, :]), waits=wv_, inc=16)
            wl = ((sGL, sGL.n),)
            G(lambda e: e.collective_compute(
                "AllReduce", ALU.max, replica_groups=[list(range(8))],
                ins=[cci.opt()], outs=[cco.opt()]),
              waits=wl, inc=1, sem_override=sCC)
            wcc = ((sCC, sCC.n),)
            GD(lambda e: e.dma_start(out=g1_sb[:, :], in_=cco),
               waits=wcc, inc=16)
            wl2 = ((sGL, sGL.n),)
            V(lambda e: e.tensor_scalar(
                sb_scal[0:1, 0:1], g1_sb[0:1, 0:1], EPS_Q, 1.0 / 255.0,
                ALU.max, ALU.mult), waits=wl2)
            vsw()
            V(lambda e: e.reciprocal(sb_scal[0:1, 1:2], sb_scal[0:1, 0:1]))
            vsw()
            wsc = snap(sV)
            T(lambda e: e.matmul(PS[0][:, 0:1], ones_f[0:1, :], sb_scal[0:1, 0:1],
                                 start=True, stop=True), waits=wsc)
            T(lambda e: e.matmul(PS[1][:, 0:1], ones_f[0:1, :], sb_scal[0:1, 1:2],
                                 start=True, stop=True))
            wb2 = snap(sT)
            V(lambda e: e.tensor_copy(sbb[:, :], PS[0][:, 0:1]), waits=wb2)
            V(lambda e: e.tensor_copy(rcpb[:, :], PS[1][:, 0:1]))

        emit_collective2(m1acc, cc1i, cc1o, srow1, rcp1b, s1b)
        barrier()
        tap("s1", lambda: srow1[:, :])

        for step in [
            lambda e: e.tensor_scalar(sv1[:, :], sw1c[:, :], s1b[:, 0:1], None, ALU.mult),
            lambda e: e.reciprocal(btmp1[:, :], sv1[:, :]),
            lambda e: e.tensor_mul(btmp3[:, :], b1_sb[:, :], btmp1[:, :]),
            lambda e: e.tensor_copy(btmp2[:, :], btmp3[:, :]),
            lambda e: e.tensor_copy(btmp4[:, :], btmp2[:, :]),
            lambda e: e.scalar_tensor_tensor(btmp1[:, :], btmp3[:, :], -0.5,
                                             btmp4[:, :], ALU.add, ALU.is_ge),
            lambda e: e.scalar_tensor_tensor(btmp3[:, :], btmp3[:, :], 0.5,
                                             btmp4[:, :], ALU.add, ALU.is_le),
            lambda e: e.tensor_sub(btmp1[:, :], btmp1[:, :], btmp3[:, :]),
            lambda e: e.tensor_add(btmp4[:, :], btmp4[:, :], btmp1[:, :]),
            lambda e: e.tensor_mul(b1q[:, :], btmp4[:, :], sv1[:, :]),
        ]:
            V(step)
            vsw()
        barrier()

        T(_noopf(), waits=wload_m1)
        evS3 = [0] * 8
        p3T = [0, 0]
        h2S = [0, 0]

        for g in range(NB * NT):
            b, t = g // NT, g % NT
            hbuf = g % 2
            GD(lambda e, b=b, t=t, hbuf=hbuf: e.dma_start(
                out=h2in.ap().rearrange("p (z c n) -> p z c n", z=2, c=DC)[:, hbuf],
                in_=h2s.rearrange("b (c p) (t n) -> b p t c n", p=128, n=TT)[b, :, t]),
               waits=((sS, h2S[hbuf]),), inc=16)
            wld = ((sGL, sGL.n),)
            for dc in range(DC):
                src = h2in[:, (hbuf * DC + dc) * TT:(hbuf * DC + dc + 1) * TT]
                S(lambda e, src=src: e.activation(uqa[:, 0:512], src, AF.Relu,
                                                  scale=rcp1b[:, 0:1]),
                  waits=wld + ((sT, p3T[hbuf]),) + snap(sV))
                ws_ = snap(sS)
                V(lambda e: e.tensor_copy(qia[:, 0:512], uqa[:, 0:512]), waits=ws_)
                V(lambda e: e.tensor_copy(qfa[:, 0:512], qia[:, 0:512]))
                V(lambda e: e.scalar_tensor_tensor(qga[:, 0:512], uqa[:, 0:512], -0.5,
                                                   qfa[:, 0:512], ALU.add, ALU.is_ge))
                V(lambda e, dc=dc, hbuf=hbuf: e.tensor_add(
                    nbuf[:, (hbuf * DC + dc) * TT:(hbuf * DC + dc + 1) * TT],
                    qfa[:, 0:512], qga[:, 0:512]))
            h2S[hbuf] = sS.n
            if g == 0:
                tap("nbuf_t0", lambda: nbuf[:, 0:DC * TT])
            wnq = snap(sV)
            for fg in range(4):
                for dc in range(DC):
                    for i in range(8):
                        fc = fg * 8 + i
                        T(lambda e, dc=dc, fc=fc, i=i, hbuf=hbuf: e.matmul(
                            PS[i][:, :],
                            m1t[:, dc * FF + fc * 128:dc * FF + fc * 128 + 128],
                            nbuf[:, (hbuf * DC + dc) * TT:(hbuf * DC + dc + 1) * TT],
                            start=(dc == 0), stop=(dc == DC - 1)),
                          waits=(wnq + ((sS, evS3[i]),) if dc == 0 else ()))
                wmm = snap(sT)
                for i in range(8):
                    fc = fg * 8 + i
                    st_i = i % 4
                    wstg = ((sSQ, sSQ.n),)
                    S(lambda e, i=i, fc=fc, st_i=st_i: e.activation(
                        astg[:, st_i * 512:(st_i + 1) * 512], PS[i][:, :], AF.Relu,
                        scale=sv1[:, fc:fc + 1], bias=b1q[:, fc:fc + 1]),
                      waits=wmm + wstg + snap(sV))
                    evS3[i] = sS.n
                    wa = snap(sS)
                    V(lambda e, st_i=st_i: e.reduce_max(
                        redt[:, :], astg[:, st_i * 512:(st_i + 1) * 512], AX.X),
                      waits=wa)
                    vsw()
                    V(lambda e: e.tensor_max(m2acc[:, :], m2acc[:, :], redt[:, :]))
                    SQD(lambda e, g=g, fc=fc, st_i=st_i: e.dma_start(
                        out=a_s[g, fc * 128:(fc + 1) * 128, :],
                        in_=astg[:, st_i * 512:(st_i + 1) * 512]),
                        waits=wa, inc=16)
            p3T[hbuf] = sT.n
        tap("m2acc", lambda: m2acc[:, :])

        emit_collective2(m2acc, cc2i, cc2o, srow1, rcp2b, s2b)
        barrier()
        tap("s2", lambda: srow1[:, :])
        for step in [
            lambda e: e.tensor_scalar(sv2[:, :], sw2c[:, :], s2b[:, 0:1], None, ALU.mult),
            lambda e: e.reciprocal(btmp1[:, 0:DC], sv2[:, :]),
            lambda e: e.tensor_mul(btmp3[:, 0:DC], b2_sb[:, :], btmp1[:, 0:DC]),
            lambda e: e.tensor_copy(btmp2[:, 0:DC], btmp3[:, 0:DC]),
            lambda e: e.tensor_copy(btmp4[:, 0:DC], btmp2[:, 0:DC]),
            lambda e: e.scalar_tensor_tensor(btmp1[:, 0:DC], btmp3[:, 0:DC], -0.5,
                                             btmp4[:, 0:DC], ALU.add, ALU.is_ge),
            lambda e: e.scalar_tensor_tensor(btmp3[:, 0:DC], btmp3[:, 0:DC], 0.5,
                                             btmp4[:, 0:DC], ALU.add, ALU.is_le),
            lambda e: e.tensor_sub(btmp1[:, 0:DC], btmp1[:, 0:DC], btmp3[:, 0:DC]),
            lambda e: e.tensor_add(btmp4[:, 0:DC], btmp4[:, 0:DC], btmp1[:, 0:DC]),
            lambda e: e.tensor_mul(b2q[:, :], btmp4[:, 0:DC], sv2[:, :]),
        ]:
            V(step)
            vsw()
        for b in range(NB):
            for ec in range(DC):
                V(lambda e, b=b, ec=ec: e.tensor_mul(
                    sv2g[:, b * DC + ec:b * DC + ec + 1],
                    sv2[:, ec:ec + 1], mcol(5, ec, b)))
                V(lambda e, b=b, ec=ec: e.tensor_mul(
                    b2qg[:, b * DC + ec:b * DC + ec + 1],
                    b2q[:, ec:ec + 1], mcol(5, ec, b)))
        barrier()

        T(_noopf(), waits=wload_m2)
        wp3st = ((sSQ, sSQ.n),)
        evP4 = [0] * 8
        ffT = [0] * 4
        for g in range(NB * NT):
            b, t = g // NT, g % NT
            GD(lambda e, b=b, t=t: e.dma_start(
                out=x1in.ap().rearrange("p (c n) -> p c n", c=DC),
                in_=x1s.rearrange("b (c p) (t n) -> b p t c n", p=128, n=TT)[b, :, t]),
               waits=snap(sV), inc=16)
            wx1ld = ((sGL, sGL.n),)
            for fc in range(FC):
                st_i = fc % 4
                SY(lambda e, g=g, fc=fc, st_i=st_i: e.dma_start(
                    out=ain[:, st_i * 512:(st_i + 1) * 512],
                    in_=a_s[g, fc * 128:(fc + 1) * 128, :]),
                   waits=snap(sS) + (wp3st if g == 0 and fc < 4 else ()), inc=16)
                wld = ((sLD, sLD.n),)
                S(lambda e, st_i=st_i: e.activation(
                    uqa[:, 0:512], ain[:, st_i * 512:(st_i + 1) * 512],
                    AF.Relu, scale=rcp2b[:, 0:1]), waits=wld + snap(sV))
                ws_ = snap(sS)
                V(lambda e: e.tensor_copy(qia[:, 0:512], uqa[:, 0:512]), waits=ws_)
                V(lambda e: e.tensor_copy(qfa[:, 0:512], qia[:, 0:512]))
                V(lambda e: e.scalar_tensor_tensor(qga[:, 0:512], uqa[:, 0:512], -0.5,
                                                   qfa[:, 0:512], ALU.add, ALU.is_ge))
                V(lambda e, st_i=st_i: e.tensor_add(
                    n2buf[:, st_i * TT:(st_i + 1) * TT],
                    qfa[:, 0:512], qga[:, 0:512]),
                  waits=((sT, ffT[st_i]),))
                wq_ = snap(sV)
                for ec in range(DC):
                    T(lambda e, fc=fc, ec=ec: e.matmul(
                        PS[ec][:, :],
                        m2t[:, fc * E + ec * 128:fc * E + ec * 128 + 128],
                        n2buf[:, (fc % 4) * TT:(fc % 4 + 1) * TT],
                        start=(fc == 0), stop=(fc == FC - 1)),
                      waits=(wq_ + ((sS, evP4[ec]),) if fc == 0 else wq_))
                ffT[fc % 4] = sT.n
            wmm = snap(sT)
            for ec in range(DC):
                r2 = ec % 2
                S(lambda e, ec=ec, b=b, r2=r2: e.activation(
                    tstg[:, r2 * 512:(r2 + 1) * 512], PS[ec][:, :], AF.Identity,
                    scale=sv2g[:, b * DC + ec:b * DC + ec + 1],
                    bias=b2qg[:, b * DC + ec:b * DC + ec + 1]),
                  waits=wmm + snap(sV))
                evP4[ec] = sS.n
                wt_ = snap(sS)
                V(lambda e, ec=ec, r2=r2: e.tensor_add(
                    x2stg[:, r2 * 512:(r2 + 1) * 512],
                    tstg[:, r2 * 512:(r2 + 1) * 512],
                    x1in[:, ec * TT:(ec + 1) * TT]),
                  waits=wt_ + wx1ld + ((sSQ, sSQ.n),))
                wx2 = snap(sV)
                SQD(lambda e, b=b, t=t, ec=ec, r2=r2: e.dma_start(
                    out=xo_h.rearrange("b (c p) (t n) -> b p t c n",
                                       p=128, n=TT)[b, :, t, ec],
                    in_=x2stg[:, r2 * 512:(r2 + 1) * 512]), waits=wx2, inc=16)

        S(_noopf(), waits=((sSQ, sSQ.n),))
        flush_all()
        ctxB.close()

    return nc


_NC_CACHE = {}


def _get_nc():
    if "nc" not in _NC_CACHE:
        _NC_CACHE["nc"] = build_program()
    return _NC_CACHE["nc"]


def kernel(**inputs):
    x = np.asarray(inputs["L0_attn_x_BSE"], dtype=np.float32)
    wq = np.ascontiguousarray(np.asarray(inputs["L0_attn_w_q"],
                                         np.float32).reshape(E, E))
    wk = np.ascontiguousarray(np.asarray(inputs["L0_attn_w_k"],
                                         np.float32).reshape(E, E))
    wv = np.ascontiguousarray(np.asarray(inputs["L0_attn_w_v"],
                                         np.float32).reshape(E, E))
    wot = np.ascontiguousarray(np.asarray(inputs["L0_attn_w_o"],
                                          np.float32).reshape(E, E).T)
    w1 = np.ascontiguousarray(np.asarray(inputs["L0_ffn_w1"], np.float32))
    w2 = np.ascontiguousarray(np.asarray(inputs["L0_ffn_w2"], np.float32))
    b1 = np.ascontiguousarray(np.asarray(inputs["L0_ffn_b1"],
                                         np.float32).reshape(FC, 128).T)
    b2 = np.ascontiguousarray(np.asarray(inputs["L0_ffn_b2"],
                                         np.float32).reshape(DC, 128).T)
    cc = np.asarray(inputs["L0_c"], np.float32)
    wmod = np.ascontiguousarray(np.asarray(inputs["L0_wmod"], np.float32))

    nc = _get_nc()
    in_maps = []
    for core in range(8):
        xs = np.ascontiguousarray(x[2 * core:2 * core + 2].transpose(0, 2, 1))
        in_maps.append({
            "x": xs, "wq": wq, "wk": wk, "wv": wv, "wot": wot,
            "w1": w1, "w2": w2, "b1": b1, "b2": b2,
            "cvec": np.ascontiguousarray(cc[2 * core:2 * core + 2]),
            "wmod": wmod,
        })
    res = run_bass_kernel_spmd(nc, in_maps, core_ids=list(range(8)))
    out = np.empty((16, SQ, E), dtype=np.float32)
    for core in range(8):
        out[2 * core:2 * core + 2] = np.asarray(
            res.results[core]["xo"]).transpose(0, 2, 1)
    return out


# version 22 (final)
# speedup vs baseline: 1.0225x; 1.0225x over previous
import numpy as np
import concourse.bass as bass
import concourse.mybir as mybir
from concourse.bass_utils import run_bass_kernel_spmd
from contextlib import ExitStack

F32 = mybir.dt.float32
BF16 = mybir.dt.bfloat16
I32 = mybir.dt.int32
AF = mybir.ActivationFunctionType
ALU = mybir.AluOpType
AX = mybir.AxisListType

E = 1024
SQ = 2048
H = 16
KD = 64
FF = 4096
DC = 8
HC = 8
FC = 32
TT = 512
NT = SQ // TT
NB = 2
VW = 65
NEG_BIG = -3.0e38
EPS_Q = 1e-8


class Sem:
    def __init__(self, ctx, nc, name):
        self.h = ctx.enter_context(nc.semaphore(name))
        self.n = 0


def build_program(taps=None):
    nc = bass.Bass("TRN2", target_bir_lowering=False)

    x_h = nc.dram_tensor("x", [NB, E, SQ], F32, kind="ExternalInput").ap()
    wq_h = nc.dram_tensor("wq", [E, E], F32, kind="ExternalInput").ap()
    wk_h = nc.dram_tensor("wk", [E, E], F32, kind="ExternalInput").ap()
    wv_h = nc.dram_tensor("wv", [E, E], F32, kind="ExternalInput").ap()
    wot_h = nc.dram_tensor("wot", [E, E], F32, kind="ExternalInput").ap()
    w1_h = nc.dram_tensor("w1", [FF, E], F32, kind="ExternalInput").ap()
    w2_h = nc.dram_tensor("w2", [E, FF], F32, kind="ExternalInput").ap()
    b1_h = nc.dram_tensor("b1", [128, FC], F32, kind="ExternalInput").ap()
    b2_h = nc.dram_tensor("b2", [128, DC], F32, kind="ExternalInput").ap()
    c_h = nc.dram_tensor("cvec", [NB, E], F32, kind="ExternalInput").ap()
    wmod_h = nc.dram_tensor("wmod", [E, 6 * E], F32, kind="ExternalInput").ap()
    xo_h = nc.dram_tensor("xo", [NB, E, SQ], F32, kind="ExternalOutput").ap()
    x1s = nc.dram_tensor("x1s", [NB, E, SQ], F32).ap()
    h2s = nc.dram_tensor("h2s", [NB, E, SQ], F32).ap()
    a_s = nc.dram_tensor("a_s", [NB * NT, FF, TT], F32).ap()
    m1s = nc.dram_tensor("m1s", [FF, E], BF16).ap()
    m2s = nc.dram_tensor("m2s", [E, FF], BF16).ap()
    cc1i = nc.dram_tensor("cc1i", [1, 128], F32).ap()
    cc1o = nc.dram_tensor("cc1o", [1, 128], F32).ap()
    cc2i = nc.dram_tensor("cc2i", [1, 128], F32).ap()
    cc2o = nc.dram_tensor("cc2o", [1, 128], F32).ap()

    tap_out = {}
    if taps:
        for name, (shape, dt) in taps.items():
            tap_out[name] = nc.dram_tensor("tap_" + name, list(shape), dt,
                                           kind="ExternalOutput").ap()

    with ExitStack() as ctx:
        sb = lambda n, s, d=F32: ctx.enter_context(nc.sbuf_tensor(n, s, d))

        mod_sb = sb("mod_sb", [128, 96])
        ones_f = sb("ones_f", [128, 128])
        ones_bf = sb("ones_bf", [128, 1], BF16)
        hsel_bf = sb("hsel_bf", [128, 2], BF16)
        idm = sb("idm", [128, 128])
        umask = sb("umask", [128, 128], BF16)
        siluT = sb("siluT", [128, DC * NB])
        cbias = sb("cbias", [128, NB * NT])
        nqacc = sb("nqacc", [2, 1])
        nkacc = sb("nkacc", [2, 1])
        ntmp = sb("ntmp", [2, 1])
        nrow = sb("nrow", [1, 4])
        uqa = sb("uqa", [128, 512])
        qia = sb("qia", [128, 512], I32)
        qfa = sb("qfa", [128, 512], BF16)
        qga = sb("qga", [128, 512], BF16)
        wst = sb("wst", [128, 512])
        mq1 = nc.alloc_sbuf_tensor_at("mq1", [128, 512], BF16,
                                      offset=nc.lookup_mloc(wst).addr)
        m1acc = sb("m1acc", [128, 1])
        m2acc = sb("m2acc", [128, 1])
        redt = sb("redt", [128, 1])
        cc_sb = uqa[0:1, 64:192]
        g1_sb = cc_sb
        srow1 = sb("srow1", [1, 2])
        ceps = sb("ceps", [128, 5])
        rcp1b = sb("rcp1b", [128, 1])
        s1b = sb("s1b", [128, 1])
        rcp2b = sb("rcp2b", [128, 1])
        s2b = sb("s2b", [128, 1])
        sw1c = sb("sw1c", [128, FC])
        sw2c = sb("sw2c", [128, DC])
        sv1 = sb("sv1", [128, FC])
        sv2 = sb("sv2", [128, DC])
        sv2g = sb("sv2g", [128, NB * DC])
        b1_sb = sb("b1_sb", [128, FC])
        b2_sb = sb("b2_sb", [128, DC])
        b1q = sb("b1q", [128, FC])
        b2q = sb("b2q", [128, DC])
        b2qg = sb("b2qg", [128, NB * DC])
        btmp1 = uqa[:, 0:FC]
        btmp2 = qia[:, 0:FC]
        btmp3 = uqa[:, FC:2 * FC]
        btmp4 = sb("btmp4", [128, FC])
        sums_row = sb("sums_row", [128, TT])
        statr = sb("statr", [1, 2 * TT])
        ctxA = ExitStack()
        sbA = lambda n, s_, d=F32: ctxA.enter_context(nc.sbuf_tensor(n, s_, d))
        wqb = sbA("wqb", [128, DC * E], BF16)
        wkb = sbA("wkb", [128, DC * E], BF16)
        wvb = sbA("wvb", [128, DC * E], BF16)
        wob = sbA("wob", [128, DC * E], BF16)
        kbuf = sbA("kbuf", [128, HC * SQ], BF16)
        vbuf = sbA("vbuf", [128, (SQ // 128) * H * VW], BF16)
        x0t = sbA("x0t", [128, 2 * DC * TT])
        hb = sbA("hb", [128, DC * TT], BF16)
        oraw = sbA("oraw", [128, HC * TT], BF16)
        qt = sbA("qt", [128, HC * TT], BF16)
        pbf = sbA("pbf", [128, 2 * TT], BF16)
        xbfc = sbA("xbfc", [128, TT], BF16)
        xsqc = sbA("xsqc", [128, TT], BF16)
        qsqc = xsqc
        ut1 = sbA("ut1", [128, TT])
        h2c = nc.alloc_sbuf_tensor_at("h2c", [128, TT], F32,
                                      offset=nc.lookup_mloc(pbf).addr)
        _kaddr = nc.lookup_mloc(kbuf).addr
        csilu = nc.alloc_sbuf_tensor_at("csilu", [NB, E], F32, offset=_kaddr)
        wmods = [nc.alloc_sbuf_tensor_at("wmods%d" % i, [128, DC * 128], F32,
                                         offset=_kaddr + 4096 * (1 + i))
                 for i in range(4)]

        PS = [ctx.enter_context(nc.psum_tensor("ps%d" % i, [128, 512], F32))
              for i in range(8)]

        sT = Sem(ctx, nc, "sT")
        sV = Sem(ctx, nc, "sV")
        sS = Sem(ctx, nc, "sS")
        sG = Sem(ctx, nc, "sG")
        sLD = Sem(ctx, nc, "sLD")
        sGL = Sem(ctx, nc, "sGL")
        sSQ = Sem(ctx, nc, "sSQ")
        sCC = Sem(ctx, nc, "sCC")

        block = ctx.enter_context(nc.Block())

        pend = {}

        def _mk(engname, sem):
            pend.setdefault(engname, [])

            def emit(f, waits=(), inc=1, sem_override=None):
                wl = [(s_.h, v) for (s_, v) in waits if v > 0]
                tgt = sem_override if sem_override is not None else sem
                real = not getattr(f, "_noop", False)
                pend[engname].append((wl, f, tgt, inc if real else 0))
                if real:
                    tgt.n += inc
                if len(pend[engname]) >= 96:
                    flush_one(engname)
            return emit

        def flush_one(engname):
            lst = pend[engname]
            if not lst:
                return
            for ent in lst:
                def body(eng, ent=ent):
                    wl, f, tgt, inc = ent
                    for h_, v_ in wl:
                        eng.wait_ge(h_, v_)
                    r = f(eng)
                    if r is not None and inc:
                        r.then_inc(tgt.h, inc)
                getattr(block, engname)(body)
            pend[engname] = []

        def flush_all():
            for engname in pend:
                flush_one(engname)

        T = _mk("tensor", sT)
        V = _mk("vector", sV)
        S = _mk("scalar", sS)
        G = _mk("gpsimd", sG)
        SY = _mk("sync", sLD)
        GD = _mk("gpsimd", sGL)
        SQD = _mk("scalar", sSQ)

        def snap(*sems):
            return tuple((s, s.n) for s in sems)

        def vsw():
            V(_noopf(), waits=snap(sV))

        def _noopf():
            f = lambda e: None
            f._noop = True
            return f

        def barrier():
            w = snap(sT, sV, sS, sG) + ((sLD, sLD.n), (sGL, sGL.n), (sSQ, sSQ.n))
            for em in (T, V, S, G):
                em(_noopf(), waits=w)

        def tap(name, apf):
            if name in tap_out:
                dst = tap_out[name]
                w = snap(sT, sV, sS, sG)
                SQD(lambda e: e.dma_start(out=dst, in_=apf()), waits=w, inc=16)
                barrier()

        mcol = lambda v, dc, b: mod_sb[:, (v * 8 + dc) * 2 + b:(v * 8 + dc) * 2 + b + 1]

        SY(lambda e: e.dma_start(out=csilu[:, :], in_=c_h), inc=16)
        SY(lambda e: e.dma_start(out=b1_sb[:, :], in_=b1_h), inc=16)
        SY(lambda e: e.dma_start(out=b2_sb[:, :], in_=b2_h), inc=16)
        GD(lambda e: e.dma_start(
            out=wqb.ap().rearrange("p (c n) -> p c n", c=DC),
            in_=wq_h.rearrange("(c p) n -> p c n", p=128)), inc=16)
        GD(lambda e: e.dma_start(
            out=wkb.ap().rearrange("p (c n) -> p c n", c=DC),
            in_=wk_h.rearrange("(c p) n -> p c n", p=128)), inc=16)
        GD(lambda e: e.dma_start(
            out=wvb.ap().rearrange("p (c n) -> p c n", c=DC),
            in_=wv_h.rearrange("(c p) n -> p c n", p=128)), inc=16)
        GD(lambda e: e.dma_start(
            out=wob.ap().rearrange("p (c n) -> p c n", c=DC),
            in_=wot_h.rearrange("(c p) n -> p c n", p=128)), inc=16)

        V(lambda e: e.memset(ones_f[:, :], 1.0))
        V(lambda e: e.memset(ones_bf[:, :], 1.0))
        V(lambda e: e.memset(hsel_bf[:, :], 0.0))
        V(lambda e: e.memset(hsel_bf[0:64, 0:1], 1.0))
        V(lambda e: e.memset(hsel_bf[64:128, 1:2], 1.0))
        V(lambda e: e.memset(idm[:, :], 1.0))
        V(lambda e: e.memset(umask[:, :], 1.0))
        V(lambda e: e.memset(ceps[:, 0:1], 1e-5))
        V(lambda e: e.memset(ceps[:, 1:2], 1e-30))
        V(lambda e: e.memset(ceps[:, 2:3], 0.5))
        V(lambda e: e.memset(ceps[:, 3:4], 128.5))
        V(lambda e: e.memset(ceps[:, 4:5], 32768.5))
        V(lambda e: e.memset(nqacc[:, :], 0.0))
        V(lambda e: e.memset(nkacc[:, :], 0.0))
        V(lambda e: e.memset(m1acc[:, :], NEG_BIG))
        V(lambda e: e.memset(m2acc[:, :], NEG_BIG))
        V(lambda e: e.memset(
            vbuf.ap().rearrange("p (m h w) -> p m h w", m=SQ // 128, h=H)[:, :, :, 64:65],
            1.0))
        G(lambda e: e.affine_select(umask[:, :], umask[:, :], [[1, 128]],
                                    ALU.is_ge, 0.0, base=0, channel_multiplier=-1),
          waits=snap(sV))
        G(lambda e: e.affine_select(idm[:, :], idm[:, :], [[1, 128]],
                                    ALU.is_equal, 0.0, base=0, channel_multiplier=-1),
          waits=snap(sV))

        S(lambda e: e.activation(csilu[:, :], csilu[:, :], AF.Silu),
          waits=((sLD, 16),))
        wsil = snap(sS, sG)
        for dc in range(DC):
            T(lambda e, dc=dc: e.transpose(
                PS[0][0:128, 0:2], csilu[0:NB, dc * 128:(dc + 1) * 128],
                idm[0:NB, 0:NB]), waits=wsil + snap(sV))
            V(lambda e, dc=dc: e.tensor_copy(siluT[:, dc * 2:dc * 2 + 2],
                                             PS[0][0:128, 0:2]), waits=snap(sT))

        wm_T = [0, 0, 0, 0]
        pm_V = [0, 0, 0, 0]
        for mc in range(48):
            sl = mc % 4
            q_, sem_ = (SY, sLD) if mc % 2 == 0 else (SQD, sSQ)
            q_(lambda e, mc=mc, sl=sl: e.dma_start(
                out=wmods[sl].ap().rearrange("p (c n) -> p c n", c=DC),
                in_=wmod_h.rearrange("(c p) (m n) -> p m c n", p=128, n=128)[:, mc]),
               waits=((sT, wm_T[sl]),) + (snap(sV) if mc < 4 else ()), inc=16)
            wl = ((sem_, sem_.n), (sV, pm_V[sl]))
            for dc in range(DC):
                T(lambda e, mc=mc, dc=dc, sl=sl: e.matmul(
                    PS[5][0:128, sl * 2:sl * 2 + 2],
                    wmods[sl][:, dc * 128:dc * 128 + 128],
                    siluT[:, dc * 2:dc * 2 + 2],
                    start=(dc == 0), stop=(dc == DC - 1)),
                  waits=(wl + (snap(sV) if mc < 4 else ()) if dc == 0 else ()))
            wm_T[sl] = sT.n
            V(lambda e, mc=mc, sl=sl: e.tensor_copy(
                mod_sb[:, mc * 2:mc * 2 + 2],
                PS[5][0:128, sl * 2:sl * 2 + 2]), waits=snap(sT))
            pm_V[sl] = sV.n
        barrier()

        V(lambda e: e.tensor_scalar_add(mod_sb[:, 16:32], mod_sb[:, 16:32], 1.0))
        V(lambda e: e.tensor_scalar_add(mod_sb[:, 64:80], mod_sb[:, 64:80], 1.0))
        for lo in (32, 80):
            sl = mod_sb[:, lo:lo + 16]
            V(lambda e, sl=sl: e.tensor_single_scalar(btmp1[:, 0:16], sl, 0.0, ALU.is_ge))
            V(lambda e: e.tensor_scalar(btmp1[:, 0:16], btmp1[:, 0:16], 2.0, -1.0,
                                        ALU.mult, ALU.add))
            V(lambda e, sl=sl: e.scalar_tensor_tensor(btmp3[:, 0:16], sl, -1.0, sl,
                                                      ALU.mult, ALU.max))
            V(lambda e: e.tensor_scalar_max(btmp3[:, 0:16], btmp3[:, 0:16], 0.25))
            V(lambda e, sl=sl: e.tensor_mul(sl, btmp1[:, 0:16], btmp3[:, 0:16]))
        barrier()
        tap("mod", lambda: mod_sb[:, :])

        wstS = [0]
        wstV = [0]
        prep_plan = []
        for fc in range(FC):
            for s_i in range(2):
                prep_plan.append(("a", w1_h, fc * 128, s_i,
                                  sw1c[:, fc:fc + 1], s_i == 0))
            prep_plan.append(("f", sw1c[:, fc:fc + 1], btmp4[:, fc:fc + 1]))
        for fc in range(FC):
            for s_i in range(2):
                prep_plan.append(("q", w1_h, fc * 128, s_i,
                                  btmp4[:, fc:fc + 1], m1s))
        for ec in range(DC):
            for s_i in range(8):
                prep_plan.append(("a", w2_h, ec * 128, s_i,
                                  sw2c[:, ec:ec + 1], s_i == 0))
            prep_plan.append(("f", sw2c[:, ec:ec + 1],
                              btmp4[:, FC - DC + ec:FC - DC + ec + 1]))
        for ec in range(DC):
            for s_i in range(8):
                prep_plan.append(("q", w2_h, ec * 128, s_i,
                                  btmp4[:, FC - DC + ec:FC - DC + ec + 1], m2s))

        def emit_prep_quantum(*ent):
            kind = ent[0]
            if kind == "f":
                out_col, rcp_col = ent[1], ent[2]
                V(lambda e: e.tensor_scalar(out_col, out_col,
                                            EPS_Q, 1.0 / 127.0, ALU.max, ALU.mult))
                vsw()
                V(lambda e: e.reciprocal(rcp_col, out_col))
                vsw()
                return
            if kind == "a":
                _, w_h, row0, s_i, out_col, first = ent
                GD(lambda e, w_h=w_h, row0=row0, s_i=s_i: e.dma_start(
                    out=wst[:, :],
                    in_=w_h[row0:row0 + 128, s_i * 512:(s_i + 1) * 512]),
                   waits=((sS, wstS[0]), (sV, wstV[0])), inc=16)
                wl = ((sGL, sGL.n),)
                V(lambda e: e.tensor_reduce(
                    redt[:, :], wst[:, :], AX.X, ALU.max,
                    apply_absolute_value=True), waits=wl)
                wstV[0] = sV.n
                vsw()
                if first:
                    V(lambda e: e.tensor_copy(out_col, redt[:, :]))
                else:
                    V(lambda e: e.tensor_max(out_col, out_col, redt[:, :]))
                return
            _, w_h, row0, s_i, rcp_col, dst = ent
            GD(lambda e, w_h=w_h, row0=row0, s_i=s_i: e.dma_start(
                out=wst[:, :],
                in_=w_h[row0:row0 + 128, s_i * 512:(s_i + 1) * 512]),
               waits=((sS, wstS[0]), (sV, wstV[0])) + ((sSQ, sSQ.n),), inc=16)
            wl = ((sGL, sGL.n),)
            S(lambda e: e.activation(uqa[:, 0:512], wst[:, :],
                                     AF.Identity, scale=rcp_col),
              waits=wl + snap(sV))
            wstS[0] = sS.n
            wS = snap(sS)
            V(lambda e: e.tensor_copy(qia[:, 0:512], uqa[:, 0:512]), waits=wS)
            V(lambda e: e.tensor_copy(qfa[:, 0:512], qia[:, 0:512]))
            V(lambda e: e.scalar_tensor_tensor(qga[:, 0:512], uqa[:, 0:512], -0.5,
                                               qfa[:, 0:512], ALU.add, ALU.is_ge))
            V(lambda e: e.scalar_tensor_tensor(qia[:, 0:512], uqa[:, 0:512], 0.5,
                                               qfa[:, 0:512], ALU.add, ALU.is_le))
            V(lambda e: e.tensor_sub(qga[:, 0:512], qga[:, 0:512], qia[:, 0:512]))
            V(lambda e: e.tensor_add(mq1[:, :], qfa[:, 0:512], qga[:, 0:512]),
              waits=((sSQ, sSQ.n),))
            wstV[0] = sV.n
            SQD(lambda e, row0=row0, s_i=s_i, dst=dst: e.dma_start(
                out=dst[row0:row0 + 128, s_i * 512:(s_i + 1) * 512],
                in_=mq1[:, :]),
                waits=snap(sV), inc=16)

        prep_iter = iter(prep_plan)

        last_stats_T = [0]
        prev_ln_done = [()]
        hb_ready = [0]
        hb_free_T = [0]
        cbound_T = [0]
        x0_free = {0: (), 1: ()}
        xload_mark = [0]
        h2mark = [0]
        psAB_V = [0]
        flashES = [0, 0]
        tailV = [0, 0]

        def xsrc(g):
            off = (g % 2) * DC * TT
            return lambda dc: x0t[:, off + dc * TT:off + (dc + 1) * TT]

        def emit_xload(g):
            b, t = divmod(g, NT)
            gen = g % 2
            SY(lambda e, b=b, t=t, gen=gen: e.dma_start(
                out=x0t.ap().rearrange("p (u c n) -> p u c n", u=2, c=DC)[:, gen],
                in_=x_h.rearrange("b (c p) (t n) -> b p t c n", p=128, n=TT)[b, :, t]),
               waits=x0_free[gen], inc=16)
            xload_mark[0] = sLD.n

        def ln_pieces(srcf, b, vs, vb, out_cb, out_done=lambda dc: None,
                      pre_out=lambda: (), first_wait=lambda: (),
                      out_gate=lambda: (), is_h=False):
            st = {"cur": 0, "bc": 0}

            def stats_chunk(dc):
                w = ((sT, last_stats_T[0] if dc == 0 else st["cur"]),)
                if dc == 0:
                    w = w + first_wait()
                V(lambda e, dc=dc: e.tensor_copy(xbfc[:, :], srcf(dc)), waits=w)
                V(lambda e, dc=dc: e.tensor_mul(xsqc[:, :], srcf(dc), srcf(dc)))
                wd = snap(sV) + (prev_ln_done[0] if dc == 0 else ())
                T(lambda e, dc=dc: e.matmul(PS[0][0:1, :], ones_bf[:, :], xbfc[:, :],
                                            start=(dc == 0), stop=(dc == DC - 1)),
                  waits=wd)
                T(lambda e, dc=dc: e.matmul(PS[1][0:1, :], ones_bf[:, :], xsqc[:, :],
                                            start=(dc == 0), stop=(dc == DC - 1)))
                st["cur"] = sT.n
                if dc == DC - 1:
                    last_stats_T[0] = sT.n

            mean_r = statr[0:1, 0:TT]
            msq_r = statr[0:1, TT:2 * TT]
            var_r = msq_r
            qn_r = mean_r

            def rows():
                wst_ = ((sT, st["cur"]),)
                V(lambda e: e.tensor_scalar_mul(mean_r, PS[0][0:1, :], 1.0 / E),
                  waits=wst_)
                vsw()
                V(lambda e: e.tensor_mul(msq_r, mean_r, mean_r))
                vsw()
                V(lambda e: e.scalar_tensor_tensor(var_r, PS[1][0:1, :], 1.0 / E,
                                                   msq_r, ALU.mult, ALU.subtract))
                vsw()
                wv_ = snap(sV)
                S(lambda e: e.activation(var_r, var_r, AF.Ln, bias=ceps[0:1, 0:1]),
                  waits=wv_)
                S(lambda e: e.activation(var_r, var_r, AF.Exp, scale=-0.5))
                wv2 = snap(sS)
                V(lambda e: e.tensor_mul(qn_r, mean_r, var_r), waits=wv2)
                vsw()
                V(lambda e: e.tensor_scalar_mul(qn_r, qn_r, -1.0))
                vsw()
                wv3 = snap(sV, sS)
                T(lambda e: e.matmul(PS[2][:, :], ones_f[0:1, :], var_r,
                                     start=True, stop=True), waits=wv3)
                T(lambda e: e.matmul(PS[3][:, :], ones_f[0:1, :], qn_r,
                                     start=True, stop=True))
                st["bc"] = sT.n

            def apply2(dc0):
                for dc in (dc0, dc0 + 1):
                    wbc = ((sT, st["bc"]),) + snap(sS)
                    V(lambda e, dc=dc: e.tensor_mul(ut1[:, :], srcf(dc),
                                                    PS[2][:, :]), waits=wbc)
                    V(lambda e: e.tensor_add(ut1[:, :], ut1[:, :], PS[3][:, :]))
                    wu = pre_out() + (out_gate() if dc == 0 else ())
                    V(lambda e, dc=dc: e.tensor_scalar(
                        ut1[:, :], ut1[:, :], mcol(vs, dc, b), None, ALU.mult),
                      waits=wu)
                    V(lambda e, dc=dc: e.tensor_scalar(
                        out_cb(dc), ut1[:, :], mcol(vb, dc, b), None, ALU.add))
                    out_done(dc)
                    if dc == DC - 1:
                        prev_ln_done[0] = snap(sV, sS)
                        if is_h:
                            hb_ready[0] = sV.n

            return ([lambda dc=dc: stats_chunk(dc) for dc in range(DC)]
                    + [rows] + [lambda d=d: apply2(d) for d in (0, 2, 4, 6)])

        def emit_qkv(g):
            b, t = divmod(g, NT)
            wh = ((sV, hb_ready[0]),)
            qkvV = {0: psAB_V[0], 1: psAB_V[0]}
            for wname, wb_ in (("q", wqb), ("k", wkb)):
                for hc in range(HC):
                    pb = PS[4 + hc % 2]
                    wfree = ((sV, qkvV[hc % 2]),)
                    for dc in range(DC):
                        T(lambda e, wb_=wb_, hc=hc, dc=dc, pb=pb: e.matmul(
                            pb[:, :],
                            wb_[:, dc * E + hc * 128:dc * E + hc * 128 + 128],
                            hb[:, dc * TT:(dc + 1) * TT],
                            start=(dc == 0), stop=(dc == DC - 1)),
                          waits=(wh + wfree if dc == 0 else ()))
                    wmm = snap(sT)
                    if wname == "q":
                        V(lambda e, hc=hc, pb=pb: e.tensor_copy(
                            qt[:, hc * TT:(hc + 1) * TT], pb[:, :]), waits=wmm)
                        qkvV[hc % 2] = sV.n
                        V(lambda e, hc=hc: e.tensor_mul(
                            qsqc[:, :], qt[:, hc * TT:(hc + 1) * TT],
                            qt[:, hc * TT:(hc + 1) * TT]))
                        wq_ = snap(sV)
                        T(lambda e: e.matmul(PS[6][0:2, :], hsel_bf[:, :],
                                             qsqc[:, :], start=True, stop=True),
                          waits=wq_)
                        wn = snap(sT)
                        V(lambda e: e.tensor_reduce(ntmp[:, :], PS[6][0:2, :],
                                                    AX.X, ALU.max), waits=wn)
                        vsw()
                        V(lambda e: e.tensor_max(nqacc[:, :], nqacc[:, :], ntmp[:, :]))
                    else:
                        V(lambda e, b=b, t=t, hc=hc, pb=pb: e.tensor_copy(
                            kbuf[:, hc * SQ + t * TT:hc * SQ + (t + 1) * TT],
                            pb[:, :]), waits=wmm)
                        qkvV[hc % 2] = sV.n
                        V(lambda e, hc=hc, b=b, t=t: e.tensor_mul(
                            qsqc[:, :], kbuf[:, hc * SQ + t * TT:hc * SQ + (t + 1) * TT],
                            kbuf[:, hc * SQ + t * TT:hc * SQ + (t + 1) * TT]))
                        wq_ = snap(sV)
                        T(lambda e: e.matmul(PS[7][0:2, :], hsel_bf[:, :],
                                             qsqc[:, :], start=True, stop=True),
                          waits=wq_)
                        wn = snap(sT)
                        V(lambda e: e.tensor_reduce(ntmp[:, :], PS[7][0:2, :],
                                                    AX.X, ALU.max), waits=wn)
                        vsw()
                        V(lambda e: e.tensor_max(nkacc[:, :], nkacc[:, :], ntmp[:, :]))
            for tsub in range(4):
                for half in range(2):
                    par = (tsub * 2 + half) % 2
                    pb = PS[4 + par]
                    wfree = ((sV, qkvV[par]),)
                    for dc in range(DC):
                        T(lambda e, tsub=tsub, half=half, dc=dc, pb=pb: e.matmul(
                            pb[:, :],
                            hb[:, dc * TT + tsub * 128:dc * TT + tsub * 128 + 128],
                            wvb[:, dc * E + half * 512:dc * E + half * 512 + 512],
                            start=(dc == 0), stop=(dc == DC - 1)),
                          waits=(wfree if dc == 0 else ()))
                    wmm = snap(sT)
                    mc = t * 4 + tsub
                    V(lambda e, mc=mc, half=half, pb=pb: e.tensor_copy(
                        vbuf.ap().rearrange(
                            "p (m h w) -> p m h w", m=SQ // 128, h=H
                        )[:, mc, half * 8:half * 8 + 8, 0:64],
                        pb.ap().rearrange("p (h k) -> p h k", h=8)),
                      waits=wmm)
                    qkvV[par] = sV.n
            hb_free_T[0] = sT.n

        def emit_cbound(g):
            b, t = divmod(g, NT)
            wnn = snap(sV)
            T(lambda e: e.transpose(PS[6][0:1, 0:2], nqacc[:, :], idm[0:2, 0:2]),
              waits=wnn)
            T(lambda e: e.transpose(PS[7][0:1, 0:2], nkacc[:, :], idm[0:2, 0:2]))
            cbound_T[0] = sT.n
            wtr = snap(sT)
            V(lambda e: e.reduce_max(nrow[0:1, 0:1], PS[6][0:1, 0:2], AX.X), waits=wtr)
            V(lambda e: e.reduce_max(nrow[0:1, 1:2], PS[7][0:1, 0:2], AX.X))
            vsw()
            wr = snap(sV)
            S(lambda e: e.activation(nrow[0:1, 2:3], nrow[0:1, 0:1], AF.Ln,
                                     bias=ceps[0:1, 1:2]), waits=wr)
            S(lambda e: e.activation(nrow[0:1, 3:4], nrow[0:1, 1:2], AF.Ln,
                                     bias=ceps[0:1, 1:2]))
            ws_ = snap(sS)
            V(lambda e: e.tensor_add(nrow[0:1, 2:3], nrow[0:1, 2:3], nrow[0:1, 3:4]),
              waits=ws_)
            vsw()
            wr2 = snap(sV)
            S(lambda e: e.activation(nrow[0:1, 2:3], nrow[0:1, 2:3], AF.Exp,
                                     scale=0.5), waits=wr2)
            ws2 = snap(sS)
            V(lambda e: e.tensor_scalar_mul(nrow[0:1, 2:3], nrow[0:1, 2:3], -0.125),
              waits=ws2)
            vsw()
            wr3 = snap(sV)
            T(lambda e: e.matmul(PS[6][:, 0:1], ones_f[0:1, :], nrow[0:1, 2:3],
                                 start=True, stop=True), waits=wr3)
            wt2 = snap(sT)
            V(lambda e, g=g: e.tensor_copy(cbias[:, g:g + 1], PS[6][:, 0:1]),
              waits=wt2)

        def emit_flash(g, sched):
            b, t = divmod(g, NT)
            nmb = 4 * (t + 1)
            fsV = sV.n
            exp_sS = {0: 0, 1: 0}
            for hc in range(HC):
                for par in range(2):
                    po = PS[4 + par]
                    h = 2 * hc + par
                    p_i = hc * 2 + par
                    wpo = snap(sS, sV)
                    tmark = {}

                    def smm(mb, hc=hc, par=par, p_i=p_i, tmark=tmark):
                        rot = mb % 2
                        w = ((sS, exp_sS[rot]),)
                        if p_i == 0 and mb <= 1:
                            w = w + ((sV, fsV),)
                        T(lambda e, mb=mb, hc=hc, par=par, rot=rot: e.matmul(
                            PS[6 + rot][:, :],
                            kbuf[par * 64:par * 64 + 64,
                                 hc * SQ + mb * 128:hc * SQ + mb * 128 + 128],
                            qt[par * 64:par * 64 + 64, hc * TT:(hc + 1) * TT],
                            start=True, stop=True), waits=w)
                        tmark[mb] = sT.n

                    def eav(mb, p_i=p_i, po=po, h=h, wpo=wpo, tmark=tmark):
                        rot = mb % 2
                        wexp = ((sT, tmark[mb]),)
                        if p_i == 0:
                            wexp = wexp + ((sV, fsV),)
                            if mb <= 1:
                                wexp = wexp + ((sSQ, h2mark[0]),)
                        S(lambda e, rot=rot, g=g: e.activation(
                            pbf[:, rot * TT:(rot + 1) * TT], PS[6 + rot][:, :],
                            AF.Exp, scale=0.125, bias=cbias[:, g:g + 1]),
                          waits=wexp)
                        exp_sS[rot] = sS.n
                        flashES[rot] = sS.n
                        wmask = snap(sS)
                        need_v = False
                        if mb >= 4 * t:
                            off = (mb - 4 * t) * 128
                            if off > 0:
                                V(lambda e, rot=rot, off=off: e.memset(
                                    pbf[:, rot * TT:rot * TT + off], 0.0),
                                  waits=wmask)
                            V(lambda e, rot=rot, off=off: e.tensor_mul(
                                pbf[:, rot * TT + off:rot * TT + off + 128],
                                pbf[:, rot * TT + off:rot * TT + off + 128],
                                umask[:, :]), waits=wmask)
                            need_v = True
                        wav = snap(sV) if need_v else snap(sS)
                        if mb == 0:
                            wav = wav + wpo
                        T(lambda e, h=h, mb=mb, rot=rot, po=po, nmb=nmb: e.matmul(
                            po[0:VW, :],
                            vbuf[:, mb * (H * VW) + h * VW:
                                 mb * (H * VW) + h * VW + VW],
                            pbf[:, rot * TT:(rot + 1) * TT],
                            start=(mb == 0), stop=(mb == nmb - 1)), waits=wav)

                    smm(0)
                    for mb in range(1, nmb):
                        smm(mb)
                        eav(mb - 1)
                    eav(nmb - 1)
                    wev = snap(sT)
                    S(lambda e, par=par, po=po: e.copy(
                        sums_row[par * 64:par * 64 + 1, :], po[64:65, :]),
                      waits=wev)
                    S(lambda e, hc=hc, par=par, po=po: e.copy(
                        oraw[par * 64:par * 64 + 64, hc * TT:(hc + 1) * TT],
                        po[0:64, :]))
                    if par == 1:
                        wsr = snap(sS)
                        T(lambda e: e.matmul(
                            PS[4][0:64, :], ones_f[0:1, 0:64],
                            sums_row[0:1, :], start=True, stop=True),
                          waits=wsr + snap(sV))
                        T(lambda e: e.matmul(
                            PS[4][64:128, :], ones_f[64:65, 0:64],
                            sums_row[64:65, :], start=True, stop=True))
                        wbc2 = snap(sT)
                        V(lambda e: e.reciprocal(PS[5][:, :], PS[4][:, :]), waits=wbc2)
                        vsw()
                        V(lambda e, hc=hc: e.tensor_mul(
                            oraw[:, hc * TT:(hc + 1) * TT],
                            oraw[:, hc * TT:(hc + 1) * TT], PS[5][:, :]))
                        psAB_V[0] = sV.n
                    for f_ in sched.get(p_i, ()):
                        f_()

        def emit_tail(g):
            b, t = divmod(g, NT)
            gen = g % 2
            off = gen * DC * TT

            def h2_done(dc, b=b, t=t):
                wd = snap(sV)
                V(lambda e: e.reduce_max(redt[:, :], h2c[:, :], AX.X), waits=wd)
                vsw()
                V(lambda e: e.tensor_max(m1acc[:, :], m1acc[:, :], redt[:, :]))
                SQD(lambda e, dc=dc, b=b, t=t: e.dma_start(
                    out=h2s.rearrange("b (c p) (t n) -> b p c t n",
                                      p=128, n=TT)[b, :, dc, t],
                    in_=h2c[:, :]), waits=wd, inc=16)

            tailT = [0]
            ln2 = ln_pieces(xsrc(g), b, 4, 3, lambda dc: h2c[:, :],
                            out_done=h2_done,
                            pre_out=lambda: ((sSQ, sSQ.n),),
                            out_gate=lambda: ((sT, tailT[0]),))

            wor = ((sV, psAB_V[0]),)
            for ec in range(DC):
                px = PS[6 + ec % 2]
                wfree = ((sS, flashES[ec % 2]), (sV, tailV[ec % 2]))
                for hc in range(HC):
                    T(lambda e, ec=ec, hc=hc, px=px: e.matmul(
                        px[:, :],
                        wob[:, hc * E + ec * 128:hc * E + ec * 128 + 128],
                        oraw[:, hc * TT:(hc + 1) * TT],
                        start=(hc == 0), stop=(hc == HC - 1)),
                      waits=(wor + wfree if hc == 0 else ()))
                wpx = snap(sT)
                V(lambda e, ec=ec, b=b, px=px, off=off: e.scalar_tensor_tensor(
                    x0t[:, off + ec * TT:off + (ec + 1) * TT], px[:, :],
                    mcol(2, ec, b),
                    x0t[:, off + ec * TT:off + (ec + 1) * TT],
                    ALU.mult, ALU.add), waits=wpx)
                tailV[ec % 2] = sV.n
                ln2[ec]()
            wx1 = snap(sV)
            SQD(lambda e, b=b, t=t, gen=gen: e.dma_start(
                out=x1s.rearrange("b (c p) (t n) -> b p t c n", p=128, n=TT)[b, :, t],
                in_=x0t.ap().rearrange("p (u c n) -> p u c n", u=2, c=DC)[:, gen]),
                waits=wx1, inc=16)
            tailT[0] = sT.n
            for f_ in ln2[8:]:
                f_()
            h2mark[0] = sSQ.n
            x0_free[gen] = snap(sV) + ((sSQ, sSQ.n),)

        NTILE = NB * NT
        PREP_SLOTS = tuple(range(16))
        NPREP_TILE = -(-len(prep_plan) // NTILE)

        def ln1_for(g):
            b2 = g // NT
            return ln_pieces(xsrc(g), b2, 1, 0,
                             lambda dc: hb[:, dc * TT:(dc + 1) * TT],
                             first_wait=lambda: ((sLD, xload_mark[0]),),
                             out_gate=lambda: ((sT, hb_free_T[0]),), is_h=True)

        emit_xload(0)
        for f_ in ln1_for(0):
            f_()
        tap("h1_t0", lambda: hb[:, :])
        for g in range(NTILE):
            b, t = divmod(g, NT)
            if t == 0 and g > 0:
                wb_ = ((sT, cbound_T[0]),)
                V(lambda e: e.memset(nqacc[:, :], 0.0), waits=wb_)
                V(lambda e: e.memset(nkacc[:, :], 0.0))
            emit_qkv(g)
            emit_cbound(g)
            sched = {}
            if g + 1 < NTILE:
                pieces = ln1_for(g + 1)
                sched[0] = [lambda g=g: emit_xload(g + 1)]
                for i_ in range(13):
                    sched.setdefault(1 + i_, []).append(pieces[i_])
            for i_ in range(NPREP_TILE):
                nxt = next(prep_iter, None)
                if nxt is None:
                    break
                p_ = PREP_SLOTS[i_ * len(PREP_SLOTS) // NPREP_TILE]
                sched.setdefault(p_, []).append(
                    lambda nxt=nxt: emit_prep_quantum(*nxt))
            emit_flash(g, sched)
            if g == 0:
                tap("oraw_t0", lambda: oraw[:, :])
            emit_tail(g)
            if g == 0:
                tap("x1_t0", lambda: x0t[:, 0:DC * TT])
                tap("h2_t0", lambda: h2c[:, 0:TT])
            if g == NTILE - 1:
                tap("h2_last", lambda: h2c[:, 0:TT])
                tap("x1_last", lambda: x0t[:, DC * TT:2 * DC * TT])
                tap("cbias", lambda: cbias[:, :])

        for nxt in prep_iter:
            emit_prep_quantum(*nxt)
        barrier()
        tap("sw1c", lambda: sw1c[:, :])
        tap("m1acc", lambda: m1acc[:, :])
        ctxA.close()
        ctxB = ExitStack()
        sbB = lambda n, s_, d=F32: ctxB.enter_context(nc.sbuf_tensor(n, s_, d))
        m1t = sbB("m1t", [128, DC * FF], BF16)
        h2in = sbB("h2in", [128, 2 * DC * TT])
        nbuf = sbB("nbuf", [128, 2 * DC * TT], BF16)
        astg = sbB("astg", [128, 4 * 512])
        m2t = sbB("m2t", [128, FC * E], BF16)
        n2buf = sbB("n2buf", [128, 4 * TT], BF16)
        ain = nc.alloc_sbuf_tensor_at("ain", [128, 4 * 512], mybir.dt.float32,
                                      offset=nc.lookup_mloc(astg).addr)
        x1in = nc.alloc_sbuf_tensor_at("x1in", [128, DC * TT], mybir.dt.float32,
                                       offset=nc.lookup_mloc(nbuf).addr)
        tstg = nc.alloc_sbuf_tensor_at("tstg", [128, 2 * 512], mybir.dt.float32,
                                       offset=nc.lookup_mloc(h2in).addr)
        x2stg = nc.alloc_sbuf_tensor_at("x2stg", [128, 2 * 512], mybir.dt.float32,
                                        offset=nc.lookup_mloc(h2in).addr + 4096)
        for dc in range(DC):
            SY(lambda e, dc=dc: e.dma_start(
                out=m1t[:, dc * FF:(dc + 1) * FF],
                in_=m1s[:, dc * 128:(dc + 1) * 128], transpose=True),
               waits=((sSQ, sSQ.n),), inc=16)
        wload_m1 = ((sLD, sLD.n),)
        for fc in range(FC):
            SY(lambda e, fc=fc: e.dma_start(
                out=m2t[:, fc * E:(fc + 1) * E],
                in_=m2s[:, fc * 128:(fc + 1) * 128], transpose=True),
               inc=16)
        wload_m2 = ((sLD, sLD.n),)

        def emit_collective2(acc, cci, cco, sb_scal, rcpb, sbb):
            T(lambda e: e.transpose(PS[2][0:1, 0:128], acc[:, :], idm[:, :]),
              waits=snap(sV))
            wtr_ = snap(sT)
            V(lambda e: e.reduce_max(cc_sb[0:1, 0:1], PS[2][0:1, 0:128], AX.X),
              waits=wtr_)
            vsw()
            V(lambda e: e.memset(cc_sb[0:1, 1:128], NEG_BIG))
            wv_ = snap(sV)
            GD(lambda e: e.dma_start(out=cci, in_=cc_sb[:, :]), waits=wv_, inc=16)
            wl = ((sGL, sGL.n),)
            G(lambda e: e.collective_compute(
                "AllReduce", ALU.max, replica_groups=[list(range(8))],
                ins=[cci.opt()], outs=[cco.opt()]),
              waits=wl, inc=1, sem_override=sCC)
            wcc = ((sCC, sCC.n),)
            GD(lambda e: e.dma_start(out=g1_sb[:, :], in_=cco),
               waits=wcc, inc=16)
            wl2 = ((sGL, sGL.n),)
            V(lambda e: e.tensor_scalar(
                sb_scal[0:1, 0:1], g1_sb[0:1, 0:1], EPS_Q, 1.0 / 255.0,
                ALU.max, ALU.mult), waits=wl2)
            vsw()
            V(lambda e: e.reciprocal(sb_scal[0:1, 1:2], sb_scal[0:1, 0:1]))
            vsw()
            wsc = snap(sV)
            T(lambda e: e.matmul(PS[0][:, 0:1], ones_f[0:1, :], sb_scal[0:1, 0:1],
                                 start=True, stop=True), waits=wsc)
            T(lambda e: e.matmul(PS[1][:, 0:1], ones_f[0:1, :], sb_scal[0:1, 1:2],
                                 start=True, stop=True))
            wb2 = snap(sT)
            V(lambda e: e.tensor_copy(sbb[:, :], PS[0][:, 0:1]), waits=wb2)
            V(lambda e: e.tensor_copy(rcpb[:, :], PS[1][:, 0:1]))

        emit_collective2(m1acc, cc1i, cc1o, srow1, rcp1b, s1b)
        barrier()
        tap("s1", lambda: srow1[:, :])

        for step in [
            lambda e: e.tensor_scalar(sv1[:, :], sw1c[:, :], s1b[:, 0:1], None, ALU.mult),
            lambda e: e.reciprocal(btmp1[:, :], sv1[:, :]),
            lambda e: e.tensor_mul(btmp3[:, :], b1_sb[:, :], btmp1[:, :]),
            lambda e: e.tensor_copy(btmp2[:, :], btmp3[:, :]),
            lambda e: e.tensor_copy(btmp4[:, :], btmp2[:, :]),
            lambda e: e.scalar_tensor_tensor(btmp1[:, :], btmp3[:, :], -0.5,
                                             btmp4[:, :], ALU.add, ALU.is_ge),
            lambda e: e.scalar_tensor_tensor(btmp3[:, :], btmp3[:, :], 0.5,
                                             btmp4[:, :], ALU.add, ALU.is_le),
            lambda e: e.tensor_sub(btmp1[:, :], btmp1[:, :], btmp3[:, :]),
            lambda e: e.tensor_add(btmp4[:, :], btmp4[:, :], btmp1[:, :]),
            lambda e: e.tensor_mul(b1q[:, :], btmp4[:, :], sv1[:, :]),
        ]:
            V(step)
            vsw()
        barrier()

        T(_noopf(), waits=wload_m1)
        evS3 = [0] * 8
        p3T = [0, 0]
        h2S = [0, 0]

        for g in range(NB * NT):
            b, t = g // NT, g % NT
            hbuf = g % 2
            GD(lambda e, b=b, t=t, hbuf=hbuf: e.dma_start(
                out=h2in.ap().rearrange("p (z c n) -> p z c n", z=2, c=DC)[:, hbuf],
                in_=h2s.rearrange("b (c p) (t n) -> b p t c n", p=128, n=TT)[b, :, t]),
               waits=((sS, h2S[hbuf]),), inc=16)
            wld = ((sGL, sGL.n),)
            for dc in range(DC):
                src = h2in[:, (hbuf * DC + dc) * TT:(hbuf * DC + dc + 1) * TT]
                S(lambda e, src=src: e.activation(uqa[:, 0:512], src, AF.Relu,
                                                  scale=rcp1b[:, 0:1]),
                  waits=wld + ((sT, p3T[hbuf]),) + snap(sV))
                ws_ = snap(sS)
                V(lambda e: e.tensor_copy(qia[:, 0:512], uqa[:, 0:512]), waits=ws_)
                V(lambda e: e.tensor_copy(qfa[:, 0:512], qia[:, 0:512]))
                V(lambda e: e.scalar_tensor_tensor(qga[:, 0:512], uqa[:, 0:512], -0.5,
                                                   qfa[:, 0:512], ALU.add, ALU.is_ge))
                V(lambda e, dc=dc, hbuf=hbuf: e.tensor_add(
                    nbuf[:, (hbuf * DC + dc) * TT:(hbuf * DC + dc + 1) * TT],
                    qfa[:, 0:512], qga[:, 0:512]))
            h2S[hbuf] = sS.n
            if g == 0:
                tap("nbuf_t0", lambda: nbuf[:, 0:DC * TT])
            wnq = snap(sV)
            for fg in range(4):
                for dc in range(DC):
                    for i in range(8):
                        fc = fg * 8 + i
                        T(lambda e, dc=dc, fc=fc, i=i, hbuf=hbuf: e.matmul(
                            PS[i][:, :],
                            m1t[:, dc * FF + fc * 128:dc * FF + fc * 128 + 128],
                            nbuf[:, (hbuf * DC + dc) * TT:(hbuf * DC + dc + 1) * TT],
                            start=(dc == 0), stop=(dc == DC - 1)),
                          waits=(wnq + ((sS, evS3[i]),) if dc == 0 else ()))
                wmm = snap(sT)
                for i in range(8):
                    fc = fg * 8 + i
                    st_i = i % 4
                    wstg = ((sSQ, sSQ.n),)
                    S(lambda e, i=i, fc=fc, st_i=st_i: e.activation(
                        astg[:, st_i * 512:(st_i + 1) * 512], PS[i][:, :], AF.Relu,
                        scale=sv1[:, fc:fc + 1], bias=b1q[:, fc:fc + 1]),
                      waits=wmm + wstg + snap(sV))
                    evS3[i] = sS.n
                    wa = snap(sS)
                    V(lambda e, st_i=st_i: e.reduce_max(
                        redt[:, :], astg[:, st_i * 512:(st_i + 1) * 512], AX.X),
                      waits=wa)
                    vsw()
                    V(lambda e: e.tensor_max(m2acc[:, :], m2acc[:, :], redt[:, :]))
                    SQD(lambda e, g=g, fc=fc, st_i=st_i: e.dma_start(
                        out=a_s[g, fc * 128:(fc + 1) * 128, :],
                        in_=astg[:, st_i * 512:(st_i + 1) * 512]),
                        waits=wa, inc=16)
            p3T[hbuf] = sT.n
        tap("m2acc", lambda: m2acc[:, :])

        emit_collective2(m2acc, cc2i, cc2o, srow1, rcp2b, s2b)
        barrier()
        tap("s2", lambda: srow1[:, :])
        for step in [
            lambda e: e.tensor_scalar(sv2[:, :], sw2c[:, :], s2b[:, 0:1], None, ALU.mult),
            lambda e: e.reciprocal(btmp1[:, 0:DC], sv2[:, :]),
            lambda e: e.tensor_mul(btmp3[:, 0:DC], b2_sb[:, :], btmp1[:, 0:DC]),
            lambda e: e.tensor_copy(btmp2[:, 0:DC], btmp3[:, 0:DC]),
            lambda e: e.tensor_copy(btmp4[:, 0:DC], btmp2[:, 0:DC]),
            lambda e: e.scalar_tensor_tensor(btmp1[:, 0:DC], btmp3[:, 0:DC], -0.5,
                                             btmp4[:, 0:DC], ALU.add, ALU.is_ge),
            lambda e: e.scalar_tensor_tensor(btmp3[:, 0:DC], btmp3[:, 0:DC], 0.5,
                                             btmp4[:, 0:DC], ALU.add, ALU.is_le),
            lambda e: e.tensor_sub(btmp1[:, 0:DC], btmp1[:, 0:DC], btmp3[:, 0:DC]),
            lambda e: e.tensor_add(btmp4[:, 0:DC], btmp4[:, 0:DC], btmp1[:, 0:DC]),
            lambda e: e.tensor_mul(b2q[:, :], btmp4[:, 0:DC], sv2[:, :]),
        ]:
            V(step)
            vsw()
        for b in range(NB):
            for ec in range(DC):
                V(lambda e, b=b, ec=ec: e.tensor_mul(
                    sv2g[:, b * DC + ec:b * DC + ec + 1],
                    sv2[:, ec:ec + 1], mcol(5, ec, b)))
                V(lambda e, b=b, ec=ec: e.tensor_mul(
                    b2qg[:, b * DC + ec:b * DC + ec + 1],
                    b2q[:, ec:ec + 1], mcol(5, ec, b)))
        barrier()

        T(_noopf(), waits=wload_m2)
        wp3st = ((sSQ, sSQ.n),)
        evP4 = [0] * 8
        ainS = [0] * 4
        ffT = [0] * 4
        for g in range(NB * NT):
            b, t = g // NT, g % NT
            GD(lambda e, b=b, t=t: e.dma_start(
                out=x1in.ap().rearrange("p (c n) -> p c n", c=DC),
                in_=x1s.rearrange("b (c p) (t n) -> b p t c n", p=128, n=TT)[b, :, t]),
               waits=snap(sV), inc=16)
            wx1ld = ((sGL, sGL.n),)
            for fc in range(FC):
                st_i = fc % 4
                SY(lambda e, g=g, fc=fc, st_i=st_i: e.dma_start(
                    out=ain[:, st_i * 512:(st_i + 1) * 512],
                    in_=a_s[g, fc * 128:(fc + 1) * 128, :]),
                   waits=((sS, ainS[st_i]),)
                         + (wp3st if g == 0 and fc < 4 else ()), inc=16)
                wld = ((sLD, sLD.n),)
                S(lambda e, st_i=st_i: e.activation(
                    uqa[:, 0:512], ain[:, st_i * 512:(st_i + 1) * 512],
                    AF.Relu, scale=rcp2b[:, 0:1]), waits=wld + snap(sV))
                ainS[st_i] = sS.n
                ws_ = snap(sS)
                V(lambda e: e.tensor_copy(qia[:, 0:512], uqa[:, 0:512]), waits=ws_)
                V(lambda e: e.tensor_copy(qfa[:, 0:512], qia[:, 0:512]))
                V(lambda e: e.scalar_tensor_tensor(qga[:, 0:512], uqa[:, 0:512], -0.5,
                                                   qfa[:, 0:512], ALU.add, ALU.is_ge))
                V(lambda e, st_i=st_i: e.tensor_add(
                    n2buf[:, st_i * TT:(st_i + 1) * TT],
                    qfa[:, 0:512], qga[:, 0:512]),
                  waits=((sT, ffT[st_i]),))
                wq_ = snap(sV)
                for ec in range(DC):
                    T(lambda e, fc=fc, ec=ec: e.matmul(
                        PS[ec][:, :],
                        m2t[:, fc * E + ec * 128:fc * E + ec * 128 + 128],
                        n2buf[:, (fc % 4) * TT:(fc % 4 + 1) * TT],
                        start=(fc == 0), stop=(fc == FC - 1)),
                      waits=(wq_ + ((sS, evP4[ec]),) if fc == 0 else wq_))
                ffT[fc % 4] = sT.n
            wmm = snap(sT)
            for ec in range(DC):
                r2 = ec % 2
                S(lambda e, ec=ec, b=b, r2=r2: e.activation(
                    tstg[:, r2 * 512:(r2 + 1) * 512], PS[ec][:, :], AF.Identity,
                    scale=sv2g[:, b * DC + ec:b * DC + ec + 1],
                    bias=b2qg[:, b * DC + ec:b * DC + ec + 1]),
                  waits=wmm + snap(sV))
                evP4[ec] = sS.n
                wt_ = snap(sS)
                V(lambda e, ec=ec, r2=r2: e.tensor_add(
                    x2stg[:, r2 * 512:(r2 + 1) * 512],
                    tstg[:, r2 * 512:(r2 + 1) * 512],
                    x1in[:, ec * TT:(ec + 1) * TT]),
                  waits=wt_ + wx1ld + ((sSQ, sSQ.n),))
                wx2 = snap(sV)
                SQD(lambda e, b=b, t=t, ec=ec, r2=r2: e.dma_start(
                    out=xo_h.rearrange("b (c p) (t n) -> b p t c n",
                                       p=128, n=TT)[b, :, t, ec],
                    in_=x2stg[:, r2 * 512:(r2 + 1) * 512]), waits=wx2, inc=16)

        S(_noopf(), waits=((sSQ, sSQ.n),))
        flush_all()
        ctxB.close()

    return nc


_NC_CACHE = {}


def _get_nc():
    if "nc" not in _NC_CACHE:
        _NC_CACHE["nc"] = build_program()
    return _NC_CACHE["nc"]


def kernel(**inputs):
    x = np.asarray(inputs["L0_attn_x_BSE"], dtype=np.float32)
    wq = np.ascontiguousarray(np.asarray(inputs["L0_attn_w_q"],
                                         np.float32).reshape(E, E))
    wk = np.ascontiguousarray(np.asarray(inputs["L0_attn_w_k"],
                                         np.float32).reshape(E, E))
    wv = np.ascontiguousarray(np.asarray(inputs["L0_attn_w_v"],
                                         np.float32).reshape(E, E))
    wot = np.ascontiguousarray(np.asarray(inputs["L0_attn_w_o"],
                                          np.float32).reshape(E, E).T)
    w1 = np.ascontiguousarray(np.asarray(inputs["L0_ffn_w1"], np.float32))
    w2 = np.ascontiguousarray(np.asarray(inputs["L0_ffn_w2"], np.float32))
    b1 = np.ascontiguousarray(np.asarray(inputs["L0_ffn_b1"],
                                         np.float32).reshape(FC, 128).T)
    b2 = np.ascontiguousarray(np.asarray(inputs["L0_ffn_b2"],
                                         np.float32).reshape(DC, 128).T)
    cc = np.asarray(inputs["L0_c"], np.float32)
    wmod = np.ascontiguousarray(np.asarray(inputs["L0_wmod"], np.float32))

    nc = _get_nc()
    in_maps = []
    for core in range(8):
        xs = np.ascontiguousarray(x[2 * core:2 * core + 2].transpose(0, 2, 1))
        in_maps.append({
            "x": xs, "wq": wq, "wk": wk, "wv": wv, "wot": wot,
            "w1": w1, "w2": w2, "b1": b1, "b2": b2,
            "cvec": np.ascontiguousarray(cc[2 * core:2 * core + 2]),
            "wmod": wmod,
        })
    res = run_bass_kernel_spmd(nc, in_maps, core_ids=list(range(8)))
    out = np.empty((16, SQ, E), dtype=np.float32)
    for core in range(8):
        out[2 * core:2 * core + 2] = np.asarray(
            res.results[core]["xo"]).transpose(0, 2, 1)
    return out
